# version 24
# baseline (speedup 1.0000x reference)
import numpy as np
from contextlib import ExitStack
import concourse.bass as bass
import concourse.mybir as mybir
import concourse.library_config as libcfg
from concourse.bass_core_rust import IndirectOffsetOnAxis

F32 = mybir.dt.float32
BF16 = mybir.dt.bfloat16
AF = mybir.ActivationFunctionType
ALU = mybir.AluOpType
AX = mybir.AxisListType

D, H, HD = 1024, 16, 64
NKV, NQ = 4096, 2048
DC = D // 128
KT = NKV // 128
NP = 8
M, E = 512, 16
EPS_RMS, EPS_LN = 1e-6, 1e-5
PSLOT = 5
NQC = NQ // 512


def build_nc():
    nc = bass.Bass("TRN2", target_bir_lowering=False)

    def I(name, shape):
        return nc.dram_tensor(name, list(shape), F32, kind="ExternalInput").ap()

    xkv = I("xkv", (NKV, D))
    Wq = I("Wq", (D, D + H))
    Wk = I("Wk", (D, D))
    Wv = I("Wv", (D, D))
    Wo = I("Wo", (D, D))
    bq_c = I("bq_c", (128, DC))
    bgate_c = I("bgate_c", (H, 1))
    bk_c = I("bk_c", (128, DC))
    bv_r = I("bv_r", (1, D))
    bo_c = I("bo_c", (128, DC))
    qn_r = I("qn_r", (1, HD))
    kn_r = I("kn_r", (1, HD))
    qne_h = I("qne_h", (2, 128))
    kne_h = I("kne_h", (2, 128))
    gwT_d = I("gwT", (D, E))
    w_gate = I("w_gate", (E, D, M))
    w_up = I("w_up", (E, D, M))
    w_down = I("w_down", (E, M, D))
    ws_gate = I("ws_gate", (D, M))
    ws_up = I("ws_up", (D, M))
    ws_down = I("ws_down", (M, D))
    ident = I("ident", (128, 128))
    ones_r = I("ones_r", (1, 512))
    e2a_d = I("e2a", (1, 128))
    e2b_d = I("e2b", (1, 128))
    eg_d = I("eg", (H, NP * 128))
    e16_d = I("e16", (E, E * 128))
    iota_d = I("iota16c", (128, E))
    h2_d = I("h2", (128, 2))
    x2i_d = nc.dram_tensor("x2i", [128, 16], mybir.dt.int32,
                           kind="ExternalInput").ap()

    y_out = nc.dram_tensor("y", [NQ, D], F32, kind="ExternalOutput").ap()
    x2_scr = nc.dram_tensor("x2_scr", [NQ + 1, D], F32, kind="Internal").ap()
    shr_d = nc.dram_tensor("shr_d", [NQ + 1, D], F32, kind="Internal").ap()
    xslot = nc.dram_tensor("xslot", [16 * 512, 1032], BF16, kind="Internal").ap()
    eoslot = nc.dram_tensor("eoslot", [16 * 512, 1024], F32, kind="Internal").ap()

    ct = {}
    def bump(s, by=1):
        ct[s] = ct.get(s, 0) + by
        return ct[s]

    ops = {"T": [], "V": [], "S": [], "G": [], "Y": []}
    sem = {}

    def em(eng, waits, body, inc=None, incby=1):
        val = bump(inc, incby) if inc else None
        def run(e, waits=tuple(waits), body=body, inc=inc, incby=incby):
            for s_, v_ in waits:
                if v_ is not None and v_ > 0:
                    e.wait_ge(sem[s_], v_)
            last = body(e)
            if inc is not None:
                assert last is not None
                last.then_inc(sem[inc], incby)
        ops[eng].append(run)
        return val

    def chain(eng, waits, steps):
        last = None
        for i, fn in enumerate(steps):
            w = (list(waits) + [(eng, ct[eng])]) if i == 0 else [(eng, ct[eng])]
            last = em(eng, w, fn, inc=eng)
        return last

    def play():
        o = {k: list(v) for k, v in ops.items()}
        for k in ops:
            ops[k] = []
        with nc.Block(no_gpsimd_drain=True) as blk:
            if o["G"]:
                blk.gpsimd(lambda g: [f(g) for f in o["G"]])
            if o["Y"]:
                blk.sync(lambda y: [f(y) for f in o["Y"]])
            if o["T"]:
                blk.tensor(lambda t: [f(t) for f in o["T"]])
            if o["S"]:
                blk.scalar(lambda s: [f(s) for f in o["S"]])
            if o["V"]:
                blk.vector(lambda v: [f(v) for f in o["V"]])

    def gdma(out_ap, in_ap, waits=(), s="gsetup"):
        return em("G", waits, lambda g, o=out_ap, i=in_ap: g.dma_start(out=o, in_=i),
                  inc=s, incby=16)

    def ydma(out_ap, in_ap, waits=(), s=None):
        return em("Y", waits, lambda y, o=out_ap, i=in_ap: y.dma_start(out=o, in_=i),
                  inc=s, incby=16)

    es = ExitStack()
    with es:
        for n in ["gsetup", "gdx0", "gdx1", "gdx2", "gdx3", "gw0", "gw1", "ge0", "ge1",
                  "xr0", "xr1", "x2w", "yr0", "yr1", "yw0", "yw1",
                  "sxp0", "sxp1", "gg0", "gg1", "gg2", "gg3", "seo", "xap0", "xap1", "gz",
                  "T", "V", "S"]:
            sem[n] = es.enter_context(nc.semaphore(n))

        id_f = es.enter_context(nc.sbuf_tensor("id_f", [128, 128], F32))
        id_b = es.enter_context(nc.sbuf_tensor("id_b", [128, 128], BF16))
        ones_b = es.enter_context(nc.sbuf_tensor("ones_b", [1, 512], BF16))
        e2a_b = es.enter_context(nc.sbuf_tensor("e2a_b", [1, 128], BF16))
        e2b_b = es.enter_context(nc.sbuf_tensor("e2b_b", [1, 128], BF16))
        eg_b = es.enter_context(nc.sbuf_tensor("eg_b", [H, NP * 128], BF16))
        h2_b = es.enter_context(nc.sbuf_tensor("h2_b", [128, 2], BF16))
        bq_cs = es.enter_context(nc.sbuf_tensor("bq_cs", [128, DC], F32))
        bk_cs = es.enter_context(nc.sbuf_tensor("bk_cs", [128, DC], F32))
        bv_b = es.enter_context(nc.sbuf_tensor("bv_b", [1, D], BF16))
        bo_cs = es.enter_context(nc.sbuf_tensor("bo_cs", [128, DC], F32))
        bgate_s = es.enter_context(nc.sbuf_tensor("bgate_s", [H, 1], F32))
        qn_row = es.enter_context(nc.sbuf_tensor("qn_row", [1, HD], F32))
        kn_row = es.enter_context(nc.sbuf_tensor("kn_row", [1, HD], F32))
        qne_b = es.enter_context(nc.sbuf_tensor("qne_b", [2, 128], BF16))
        kne_b = es.enter_context(nc.sbuf_tensor("kne_b", [2, 128], BF16))
        tiny = es.enter_context(nc.sbuf_tensor("tiny", [1, 8], F32))
        tinyb = es.enter_context(nc.sbuf_tensor("tinyb", [1, 8], BF16))
        es_og = ExitStack()
        OgT = es_og.enter_context(nc.sbuf_tensor("OgT", [128, DC * NQ], BF16))

        es01 = ExitStack()
        xT = es01.enter_context(nc.sbuf_tensor("xT", [128, DC * NKV], BF16))
        Wg_s = es01.enter_context(nc.sbuf_tensor("Wg_s", [128, DC * H], BF16))
        Wq_p = es01.enter_context(nc.sbuf_tensor("Wq_p", [128, 2 * DC * 128], BF16))
        Wk_p = es01.enter_context(nc.sbuf_tensor("Wk_p", [128, 2 * DC * 128], BF16))
        Wv_p = es01.enter_context(nc.sbuf_tensor("Wv_p", [128, 2 * DC * 128], BF16))
        xtile = es01.enter_context(nc.sbuf_tensor("xtile", [128, 4 * D], BF16))
        K_sb = es01.enter_context(nc.sbuf_tensor("K_sb", [128, 2 * NKV], BF16))
        Q_sb = es01.enter_context(nc.sbuf_tensor("Q_sb", [128, 2 * NQ], BF16))
        Vp = es01.enter_context(nc.sbuf_tensor("Vp", [128, 2 * KT * 130], BF16))
        Pring = es01.enter_context(nc.sbuf_tensor("Pring", [128, PSLOT * 1024], BF16))
        Osb = es01.enter_context(nc.sbuf_tensor("Osb", [128, NQ], BF16))
        se_fl = es01.enter_context(nc.sbuf_tensor("se_fl", [1, 2 * NQ], BF16))
        Gden = es01.enter_context(nc.sbuf_tensor("Gden", [128, 2 * 512], F32))
        GdenR = es01.enter_context(nc.sbuf_tensor("GdenR", [128, 2 * 512], BF16))
        Gq_sb = es01.enter_context(nc.sbuf_tensor("Gq_sb", [128, 2 * 512], BF16))
        sgate = es01.enter_context(nc.sbuf_tensor("sgate", [H, NQ], BF16))
        ecol = es01.enter_context(nc.sbuf_tensor("ecol", [128, 1], F32))
        ksq = es01.enter_context(nc.sbuf_tensor("ksq", [128, 512], BF16))
        kb_sb = es01.enter_context(nc.sbuf_tensor("kb_sb", [128, 512], BF16))
        ss1 = es01.enter_context(nc.sbuf_tensor("ss1", [2, 512], F32))
        ss2 = es01.enter_context(nc.sbuf_tensor("ss2", [2, 512], F32))
        rrow = es01.enter_context(nc.sbuf_tensor("rrow", [2, 512], BF16))
        rexp = es01.enter_context(nc.sbuf_tensor("rexp", [128, 512], BF16))

        psP = es01.enter_context(nc.psum_tensor("psP", [128, 2 * 512], F32))
        psS = es01.enter_context(nc.psum_tensor("psS", [128, 2 * 1024], F32))
        psO = es01.enter_context(nc.psum_tensor("psO", [128, 2 * 512], F32))
        psX = psS[:].bitcast(BF16)

        gdma(id_f[:], ident)
        gdma(id_b[:], ident)
        gdma(ones_b[:], ones_r)
        gdma(e2a_b[:], e2a_d)
        gdma(e2b_b[:], e2b_d)
        gdma(eg_b[:], eg_d)
        gdma(h2_b[:], h2_d)
        gdma(bq_cs[:], bq_c)
        gdma(bk_cs[:], bk_c)
        gdma(bv_b[:], bv_r)
        gdma(bo_cs[:], bo_c)
        gdma(bgate_s[:], bgate_c)
        gdma(qn_row[:], qn_r)
        gdma(kn_row[:], kn_r)
        gdma(qne_b[:], qne_h)
        gdma(kne_b[:], kne_h)
        gdma(Wg_s[:].rearrange("p (c j) -> p c j", c=DC),
             Wq.rearrange("(c p) j -> p c j", p=128)[:, :, D:D + H])
        n_setup = ct["gsetup"]
        zzx = OgT[:, 0:1032]
        v_zzx = em("V", [], lambda v: v.memset(zzx, 0.0), inc="V")
        for zi in range(64):
            ydma(xslot[zi * 128:(zi + 1) * 128, :], zzx,
                 waits=[("V", v_zzx)] if zi == 0 else [], s="gz")
        n_gz = ct["gz"]

        v_bias = chain("V", [("gsetup", n_setup)], [
            lambda v: v.tensor_reduce(tiny[:, 0:1], qn_row[:], AX.X, ALU.max,
                                      apply_absolute_value=True),
            lambda v: v.tensor_reduce(tiny[:, 1:2], kn_row[:], AX.X, ALU.max,
                                      apply_absolute_value=True),
            lambda v: v.tensor_tensor(tiny[:, 2:3], tiny[:, 0:1], tiny[:, 1:2], ALU.mult),
            lambda v: v.tensor_scalar(tiny[:, 3:4], tiny[:, 2:3], -8.0, -2.0,
                                      ALU.mult, ALU.add),
            lambda v: v.tensor_copy(tinyb[:, 0:1], tiny[:, 3:4]),
        ])
        t_bias = em("T", [("V", v_bias)], lambda t: t.matmul(
            psO[:, 0:1], ones_b[0:1, 0:128], tinyb[0:1, 0:1], start=True, stop=True),
            inc="T")
        em("V", [("T", t_bias)], lambda v: v.tensor_copy(ecol[:], psO[:, 0:1]), inc="V")
        em("V", [], lambda v: v.memset(
            Vp[:].rearrange("p (t two c) -> p t two c", two=2, c=65)[:, :, :, 64:65], 1.0),
            inc="V")

        t_xtr, v_xev = {}, {}
        for i in range(KT):
            buf = i % 4
            w = [("T", t_xtr[i - 4])] if i - 4 in t_xtr else []
            nld = gdma(xtile[:, buf * D:(buf + 1) * D],
                       xkv[i * 128:(i + 1) * 128, :], waits=w, s=f"gdx{buf}")
            def xt_body(t, buf=buf):
                r = None
                for d in range(DC):
                    r = t.transpose(psX[:, (buf % 2) * D + d * 128:(buf % 2) * D + (d + 1) * 128],
                                    xtile[:, buf * D + d * 128:buf * D + (d + 1) * 128],
                                    id_b[:])
                return r
            wt = [(f"gdx{buf}", nld)]
            if i - 2 in v_xev:
                wt.append(("V", v_xev[i - 2]))
            t_xtr[i] = em("T", wt, xt_body, inc="T")
            v_xev[i] = em("V", [("T", t_xtr[i])], lambda v, i=i, buf=buf: v.tensor_copy(
                xT[:].rearrange("p (c n) -> p c n", c=DC)[:, :, i * 128:(i + 1) * 128],
                psX[:, (buf % 2) * D:((buf % 2) + 1) * D].rearrange("p (c n) -> p c n", c=DC)),
                inc="V")
        v_x_done = v_xev[KT - 1]

        gate_tiles = [psP[0:H, 0:512], psP[0:H, 512:1024],
                      psO[0:H, 0:512], psO[0:H, 512:1024]]
        for qc in range(NQC):
            def g_body(t, qc=qc):
                r = None
                for d in range(DC):
                    r = t.matmul(gate_tiles[qc],
                                 Wg_s[:, d * H:(d + 1) * H],
                                 xT[:, d * NKV + qc * 512:d * NKV + qc * 512 + 512],
                                 start=(d == 0), stop=(d == DC - 1))
                return r
            t_gate = em("T", [("V", v_x_done)], g_body, inc="T")
            em("S", [("T", t_gate)], lambda s, qc=qc: s.activation(
                sgate[:, qc * 512:(qc + 1) * 512], gate_tiles[qc],
                AF.Sigmoid, bias=bgate_s[:]), inc="S")

        play()

        proj_done = {}
        proj_seq = [0]
        proj_T_end = {}
        t_pv_g = {}
        s_exp_g = {}
        gidx = [0]
        nW = {}

        PSL = [psP[:, 0:512], psP[:, 512:1024]]

        def proj_units(c):
            wb = c % 2
            wsem = f"gw{wb}"
            n_w = nW[c]
            woff = wb * DC * 128
            koff = wb * NKV
            qoff = wb * NQ
            voff = wb * KT * 130
            for kind in ("k", "q"):
                if kind == "k":
                    W_s, dst, dof, tcs, bcol, ne = Wk_p, K_sb, koff, NKV // 512, bk_cs, kne_b
                else:
                    W_s, dst, dof, tcs, bcol, ne = Wq_p, Q_sb, qoff, NQ // 512, bq_cs, qne_b
                for tc in range(tcs):
                    seq = proj_seq[0]; proj_seq[0] += 1
                    X = PSL[seq % 2]
                    Y = X
                    wt = [(wsem, n_w)]
                    if seq - 2 in proj_done:
                        wt.append(("V", proj_done[seq - 2]))
                    def kmm(t, X=X, woff=woff, tc=tc, W_s=W_s):
                        r = None
                        for d in range(DC):
                            r = t.matmul(X,
                                         W_s[:, woff + d * 128:woff + (d + 1) * 128],
                                         xT[:, d * NKV + tc * 512:d * NKV + tc * 512 + 512],
                                         start=(d == 0), stop=(d == DC - 1))
                        return r
                    t_mm = em("T", wt, kmm, inc="T")
                    v_kb = chain("V", [("T", t_mm)], [
                        lambda v, X=X, bcol=bcol, c=c: v.tensor_scalar(
                            kb_sb[:, 0:512], X, bcol[:, c:c + 1], None, ALU.add),
                        lambda v: v.tensor_tensor(ksq[:, 0:512], kb_sb[:, 0:512],
                                                  kb_sb[:, 0:512], ALU.mult),
                    ])
                    yield
                    yield
                    t_ss = em("T", [("V", v_kb)], lambda t, Y=Y: t.matmul(
                        Y[0:2, :], h2_b[:], ksq[:, 0:512], start=True, stop=True),
                        inc="T")
                    v_st = em("V", [("T", t_ss)], lambda v, Y=Y: v.tensor_scalar(
                        ss1[:, 0:512], Y[0:2, :],
                        1.0 / HD, EPS_RMS, ALU.mult, ALU.add), inc="V")
                    s_ln = em("S", [("V", v_st)], lambda s: s.activation(
                        ss2[:, 0:512], ss1[:, 0:512], AF.Ln), inc="S")
                    s_ex = em("S", [("S", s_ln)], lambda s: s.activation(
                        rrow[:, 0:512], ss2[:, 0:512], AF.Exp, scale=-0.5),
                        inc="S")
                    yield
                    yield
                    t_rx = em("T", [("S", s_ex)], lambda t, Y=Y, ne=ne: t.matmul(
                        Y, ne[:], rrow[:, 0:512], start=True, stop=True), inc="T")
                    proj_done[seq] = chain("V", [("T", t_rx)], [
                        lambda v, Y=Y: v.tensor_copy(rexp[:, 0:512], Y),
                        lambda v, tc=tc, dst=dst, dof=dof: v.tensor_tensor(
                            dst[:, dof + tc * 512:dof + (tc + 1) * 512],
                            kb_sb[:, 0:512], rexp[:, 0:512], ALU.mult),
                    ])
                    yield
            for tt in range(KT):
                seq = proj_seq[0]; proj_seq[0] += 1
                X = PSL[seq % 2]
                wt = [(wsem, n_w)]
                if seq - 2 in proj_done:
                    wt.append(("V", proj_done[seq - 2]))
                def vmm(t, X=X, woff=woff, tt=tt, c=c):
                    for d in range(DC):
                        t.matmul(X[:, 0:128],
                                 xT[:, d * NKV + tt * 128:d * NKV + (tt + 1) * 128],
                                 Wv_p[:, woff + d * 128:woff + (d + 1) * 128],
                                 start=(d == 0), stop=False)
                    return t.matmul(X[:, 0:128], ones_b[0:1, 0:128],
                                    bv_b[0:1, c * 128:(c + 1) * 128],
                                    start=False, stop=True)
                t_mm = em("T", wt, vmm, inc="T")
                proj_done[seq] = em("V", [("T", t_mm)], lambda v, X=X, tt=tt, voff=voff:
                    v.tensor_copy(
                        Vp[:, voff + tt * 130:voff + tt * 130 + 130].rearrange(
                            "p (two c) -> p two c", two=2)[:, :, 0:64],
                        X[:, 0:128].rearrange("p (two c) -> p two c", two=2)),
                    inc="V")
                yield

        def weight_dmas(c):
            wb = c % 2
            wwait = [("T", proj_T_end[c - 2])] if c - 2 in proj_T_end else []
            gdma(Wq_p[:, wb * DC * 128:(wb + 1) * DC * 128].rearrange(
                "p (d j) -> p d j", d=DC),
                Wq.rearrange("(d p) j -> p d j", p=128)[:, :, c * 128:(c + 1) * 128],
                waits=wwait, s=f"gw{wb}")
            gdma(Wk_p[:, wb * DC * 128:(wb + 1) * DC * 128].rearrange(
                "p (d j) -> p d j", d=DC),
                Wk.rearrange("(d p) j -> p d j", p=128)[:, :, c * 128:(c + 1) * 128],
                s=f"gw{wb}")
            nW[c] = gdma(Wv_p[:, wb * DC * 128:(wb + 1) * DC * 128].rearrange(
                "p (d j) -> p d j", d=DC),
                Wv.rearrange("(d p) j -> p d j", p=128)[:, :, c * 128:(c + 1) * 128],
                s=f"gw{wb}")

        weight_dmas(0)
        weight_dmas(1)
        for _ in proj_units(0):
            pass
        proj_T_end[0] = ct["T"]
        v_proj_done_pair = {0: proj_done[proj_seq[0] - 1]}
        v_og_last = {}

        for c in range(NP):
            kb = (c % 2) * NKV
            qb = (c % 2) * NQ
            vb = (c % 2) * KT * 130
            if c + 1 < NP:
                if c + 1 >= 2:
                    weight_dmas(c + 1)
                pg = proj_units(c + 1)
            else:
                pg = iter(())
            pg_done = False

            v_oev = {}
            pend_pv = []

            def make_pv(gi2, kt2, slot2, qc2, c=c, vb=vb):
                wp = [("S", s_exp_g[gi2])]
                if kt2 == 0 and qc2 - 1 in v_oev:
                    wp.append(("V", v_oev[qc2 - 1]))
                if kt2 == 0 and qc2 == 0 and c - 1 in v_og_last:
                    wp.append(("V", v_og_last[c - 1]))
                def pv(t, kt=kt2, slot=slot2, vb=vb):
                    t.matmul(psO[0:65, 0:512],
                             Vp[:, vb + kt * 130:vb + kt * 130 + 65],
                             Pring[:, slot * 1024:slot * 1024 + 512],
                             start=(kt == 0), stop=(kt == KT - 1))
                    return t.matmul(psO[0:65, 512:1024],
                                    Vp[:, vb + kt * 130 + 65:vb + kt * 130 + 130],
                                    Pring[:, slot * 1024 + 512:(slot + 1) * 1024],
                                    start=(kt == 0), stop=(kt == KT - 1))
                t_pv_g[gi2] = em("T", wp, pv, inc="T")

            for qc in range(NQC):
                for kt in range(KT):
                    gi = gidx[0]; gidx[0] += 1
                    buf = gi % 2
                    slot = gi % PSLOT
                    wt = []
                    if gi - 2 in s_exp_g:
                        wt.append(("S", s_exp_g[gi - 2]))
                    if kt == 0 and qc == 0:
                        wt.append(("V", v_proj_done_pair[c]))
                        if c - 1 in v_og_last:
                            wt.append(("V", v_og_last[c - 1]))
                    def smm(t, buf=buf, kt=kt, qc=qc, kb=kb, qb=qb):
                        t.matmul(psS[:, buf * 1024:buf * 1024 + 512],
                                 K_sb[0:64, kb + kt * 128:kb + (kt + 1) * 128],
                                 Q_sb[0:64, qb + qc * 512:qb + (qc + 1) * 512],
                                 start=True, stop=True, tile_position=(0, 0))
                        return t.matmul(psS[:, buf * 1024 + 512:buf * 1024 + 1024],
                                        K_sb[64:128, kb + kt * 128:kb + (kt + 1) * 128],
                                        Q_sb[64:128, qb + qc * 512:qb + (qc + 1) * 512],
                                        start=True, stop=True, tile_position=(64, 0))
                    t_s = em("T", wt, smm, inc="T")
                    we = [("T", t_s)]
                    s_exp_g[gi] = em("S", we, lambda s, buf=buf, slot=slot: s.activation(
                        Pring[:, slot * 1024:(slot + 1) * 1024],
                        psS[:, buf * 1024:(buf + 1) * 1024],
                        AF.Exp, bias=ecol[:], scale=0.125), inc="S")
                    pend_pv.append((gi, kt, slot, qc))
                    if len(pend_pv) > 2:
                        make_pv(*pend_pv.pop(0))
                    if not pg_done:
                        try:
                            next(pg)
                        except StopIteration:
                            pg_done = True
                            proj_T_end[c + 1] = ct["T"]
                            v_proj_done_pair[c + 1] = proj_done[proj_seq[0] - 1]
                while pend_pv:
                    make_pv(*pend_pv.pop(0))
                def oev(v, qc=qc):
                    v.tensor_copy(Osb[0:64, qc * 512:(qc + 1) * 512], psO[0:64, 0:512])
                    v.tensor_copy(se_fl[0:1, qc * 512:(qc + 1) * 512],
                                  psO[64:65, 0:512])
                    v.tensor_copy(Osb[64:128, qc * 512:(qc + 1) * 512],
                                  psO[0:64, 512:1024])
                    return v.tensor_copy(se_fl[0:1, NQ + qc * 512:NQ + (qc + 1) * 512],
                                         psO[64:65, 512:1024])
                v_oev[qc] = em("V", [("T", t_pv_g[gidx[0] - 1])], oev, inc="V")
            if not pg_done:
                for _ in pg:
                    pass
                proj_T_end[c + 1] = ct["T"]
                v_proj_done_pair[c + 1] = proj_done[proj_seq[0] - 1]

            v_og = {}
            GNUM = [psO[:, 0:512], psS[:, 0:512]]
            GDEN = [psO[:, 512:1024], psS[:, 512:1024]]
            for tc in range(NQC):
                gb = tc % 2
                wt = [("V", v_oev[NQC - 1])]
                if c == 0 and tc == 0:
                    wt.append(("gz", n_gz))
                if tc - 2 in v_og:
                    wt.append(("V", v_og[tc - 2]))
                t_gn = em("T", wt, lambda t, tc=tc, c=c, gb=gb: t.matmul(
                    GNUM[gb], eg_b[:, c * 128:(c + 1) * 128],
                    sgate[0:H, tc * 512:(tc + 1) * 512], start=True, stop=True),
                    inc="T")
                def gden_mm(t, tc=tc, gb=gb):
                    t.matmul(GDEN[gb], e2a_b[:],
                             se_fl[0:1, tc * 512:(tc + 1) * 512],
                             start=True, stop=False)
                    return t.matmul(GDEN[gb], e2b_b[:],
                                    se_fl[0:1, NQ + tc * 512:NQ + (tc + 1) * 512],
                                    start=False, stop=True)
                t_gd = em("T", [("T", t_gn)], gden_mm, inc="T")
                s_gl = em("S", [("T", t_gd)], lambda s, gb=gb: s.activation(
                    Gden[:, gb * 512:(gb + 1) * 512], GDEN[gb], AF.Ln), inc="S")
                s_ge = em("S", [("S", s_gl)], lambda s, gb=gb: s.activation(
                    GdenR[:, gb * 512:(gb + 1) * 512],
                    Gden[:, gb * 512:(gb + 1) * 512], AF.Exp, scale=-1.0), inc="S")
                v_og[tc] = chain("V", [("S", s_ge)], [
                    lambda v, gb=gb: v.tensor_tensor(
                        Gq_sb[:, gb * 512:(gb + 1) * 512], GNUM[gb],
                        GdenR[:, gb * 512:(gb + 1) * 512], ALU.mult),
                    lambda v, tc=tc, c=c, gb=gb: v.tensor_tensor(
                        OgT[:, c * NQ + tc * 512:c * NQ + (tc + 1) * 512],
                        Osb[:, tc * 512:(tc + 1) * 512],
                        Gq_sb[:, gb * 512:(gb + 1) * 512], ALU.mult),
                ])
            v_og_last[c] = max(v_og[NQC - 1], v_og[NQC - 2])
        play()
        es01.close()
        es_r = ExitStack()
        x2T = es_r.enter_context(nc.sbuf_tensor("x2T", [128, DC * NQ], BF16, side="right"))

        es2 = ExitStack()
        Wo_s = es2.enter_context(nc.sbuf_tensor("Wo_s", [128, DC * D], BF16))
        Zt = es2.enter_context(nc.sbuf_tensor("Zt", [128, DC * 512], BF16))
        xres = es2.enter_context(nc.sbuf_tensor("xres", [128, 2 * D], F32))
        pre1 = es2.enter_context(nc.sbuf_tensor("pre1", [128, D], F32))
        x2t = es2.enter_context(nc.sbuf_tensor("x2t", [128, D], F32))
        x2b = es2.enter_context(nc.sbuf_tensor("x2b", [128, D], BF16))
        sqj = es2.enter_context(nc.sbuf_tensor("sqj", [128, D], BF16))
        st_ = es2.enter_context(nc.sbuf_tensor("st_", [128, 16], F32))
        psZ = es2.enter_context(nc.psum_tensor("psZ", [128, 2 * 512], F32))
        psT2 = es2.enter_context(nc.psum_tensor("psT2", [128, 2 * 1024], BF16))
        psU2 = es2.enter_context(nc.psum_tensor("psU2", [128, 2 * 1024], BF16))

        s_zev, v_pre, v_x2e, t_x2t_h, t_tok_last = {}, {}, {}, {}, {}
        s_rs_h, nx2w_h = {}, {}
        nWo = gdma(Wo_s[:].rearrange("p (c j) -> p c j", c=DC),
                   Wo.rearrange("(c p) j -> p c j", p=128))
        v_z0 = em("V", [], lambda v: v.memset(x2t[:], 0.0), inc="V")
        nx2w_h[-1] = ydma(x2_scr[0:1, :], x2t[0:1, :], waits=[("V", v_z0)], s="x2w")
        zseq = [0]
        for tc in range(4):
            for dp in range(DC):
                seq = zseq[0]; zseq[0] += 1
                buf = seq % 2
                wt = [("gsetup", nWo)]
                if seq - 2 in s_zev:
                    wt.append(("S", s_zev[seq - 2]))
                def zmm(t, buf=buf, dp=dp, tc=tc):
                    r = None
                    for d in range(DC):
                        r = t.matmul(psZ[:, buf * 512:(buf + 1) * 512],
                                     Wo_s[:, d * D + dp * 128:d * D + (dp + 1) * 128],
                                     OgT[:, d * NQ + tc * 512:d * NQ + (tc + 1) * 512],
                                     start=(d == 0), stop=(d == DC - 1))
                    return r
                t_z = em("T", wt, zmm, inc="T")
                wz = [("T", t_z)]
                if tc - 1 in t_tok_last:
                    wz.append(("T", t_tok_last[tc - 1]))
                s_zev[seq] = em("S", wz, lambda s, buf=buf, dp=dp: s.activation(
                    Zt[:, dp * 512:(dp + 1) * 512], psZ[:, buf * 512:(buf + 1) * 512],
                    AF.Identity, bias=bo_cs[:, dp:dp + 1]), inc="S")
            s_z_last = s_zev[zseq[0] - 1]
            for q in range(4):
                g = tc * 4 + q
                buf = g % 2
                wx = [("V", v_pre[g - 2])] if g - 2 in v_pre else []
                nxr = ydma(xres[:, buf * D:(buf + 1) * D],
                           xkv[g * 128:(g + 1) * 128, :], waits=wx, s=f"xr{buf}")
                wt = [("S", s_z_last)]
                if g - 2 in v_pre:
                    wt.append(("V", v_pre[g - 2]))
                def ttok(t, buf=buf, q=q):
                    r = None
                    for dp in range(DC):
                        r = t.transpose(
                            psT2[:, buf * 1024 + dp * 128:buf * 1024 + (dp + 1) * 128],
                            Zt[:, dp * 512 + q * 128:dp * 512 + (q + 1) * 128],
                            id_b[:])
                    return r
                t_tok = em("T", wt, ttok, inc="T")
                t_tok_last[tc] = t_tok
                v_pre[g] = em("V", [("T", t_tok), (f"xr{buf}", nxr)],
                              lambda v, buf=buf: v.tensor_tensor(
                                  pre1[:], psT2[:, buf * 1024:(buf + 1) * 1024],
                                  xres[:, buf * D:(buf + 1) * D], ALU.add), inc="V")
                v_st = chain("V", [("V", v_pre[g])], [
                    lambda v: v.tensor_reduce(st_[:, 0:1], pre1[:], AX.X, ALU.add,
                                              negate=True),
                    lambda v: v.scalar_tensor_tensor(sqj[:], pre1[:], 1.0, pre1[:],
                                                     ALU.mult, ALU.mult,
                                                     accum_out=st_[:, 2:3]),
                    lambda v: v.tensor_scalar(st_[:, 1:2], st_[:, 0:1], 1.0 / D,
                                              None, ALU.mult),
                    lambda v: v.tensor_scalar(st_[:, 3:4], st_[:, 2:3], 1.0 / D,
                                              None, ALU.mult),
                    lambda v: v.tensor_tensor(st_[:, 4:5], st_[:, 1:2], st_[:, 1:2],
                                              ALU.mult),
                    lambda v: v.tensor_tensor(st_[:, 5:6], st_[:, 3:4], st_[:, 4:5],
                                              ALU.subtract),
                    lambda v: v.tensor_scalar(st_[:, 6:7], st_[:, 5:6], 1.0, EPS_LN,
                                              ALU.mult, ALU.add),
                    lambda v: v.reciprocal(st_[:, 7:8], st_[:, 6:7]),
                ])
                s_rs = em("S", [("V", v_st)], lambda s: s.activation(
                    st_[:, 8:9], st_[:, 7:8], AF.Sqrt), inc="S")
                s_rs_h[g] = s_rs
                wx2 = [("S", s_rs)]
                if g - 1 in nx2w_h:
                    wx2.append(("x2w", nx2w_h[g - 1]))
                v_x2 = chain("V", wx2, [
                    lambda v: v.tensor_tensor(st_[:, 9:10], st_[:, 1:2], st_[:, 8:9],
                                              ALU.mult),
                    lambda v: v.scalar_tensor_tensor(
                        x2t[:], pre1[:], st_[:, 8:9],
                        st_[:, 9:10].broadcast_to([128, D]), ALU.mult, ALU.add),
                ])
                nx2w_h[g] = ydma(x2_scr[1 + g * 128:1 + (g + 1) * 128, :], x2t[:],
                                 waits=[("V", v_x2)], s="x2w")
                wcb = [("V", v_x2)]
                if g - 1 in t_x2t_h:
                    wcb.append(("T", t_x2t_h[g - 1]))
                v_cb = em("V", wcb, lambda v: v.tensor_copy(x2b[:], x2t[:]), inc="V")
                wt2 = [("V", v_cb)]
                if g - 2 in v_x2e:
                    wt2.append(("V", v_x2e[g - 2]))
                def tx2(t, buf=buf):
                    r = None
                    for d in range(DC):
                        r = t.transpose(
                            psU2[:, buf * 1024 + d * 128:buf * 1024 + (d + 1) * 128],
                            x2b[:, d * 128:(d + 1) * 128], id_b[:])
                    return r
                t_x2t_h[g] = em("T", wt2, tx2, inc="T")
                v_x2e[g] = em("V", [("T", t_x2t_h[g])], lambda v, g=g, buf=buf:
                    v.tensor_copy(
                        x2T[:].rearrange("p (c n) -> p c n", c=DC)[:, :, g * 128:(g + 1) * 128],
                        psU2[:, buf * 1024:(buf + 1) * 1024].rearrange(
                            "p (c n) -> p c n", c=DC)), inc="V")
        em("Y", [("x2w", ct.get("x2w", 0)), ("xr0", ct.get("xr0", 0)),
                 ("xr1", ct.get("xr1", 0))], lambda y: None)
        em("G", [("gsetup", ct.get("gsetup", 0))], lambda g: None)
        play()
        es2.close()
        es_og.close()

        CAP = 512
        CT = CAP // 128
        es3 = ExitStack()
        e16_b = es3.enter_context(nc.sbuf_tensor("e16_b", [E, E * 128], BF16))
        wgu = es3.enter_context(nc.sbuf_tensor("wgu", [128, 2 * 2 * DC * 512], BF16))
        wdn = es3.enter_context(nc.sbuf_tensor("wdn", [128, 2 * 4 * 1024], BF16))
        rlx = es3.enter_context(nc.sbuf_tensor("rlx", [128, 2 * CAP], BF16))
        rowb = es3.enter_context(nc.sbuf_tensor("rowb", [1, 2 * CAP], BF16))
        silu_g = es3.enter_context(nc.sbuf_tensor("silu_g", [128, 2 * 1024], BF16))
        u_p = es3.enter_context(nc.sbuf_tensor("u_p", [128, 2 * 1024], BF16))
        h_p = es3.enter_context(nc.sbuf_tensor("h_p", [128, 2 * 1024], BF16))
        xgb = es3.enter_context(nc.sbuf_tensor("xgb", [128, 4 * 1040], BF16))
        xgT = es3.enter_context(nc.sbuf_tensor("xgT", [128, 2 * DC * CAP], BF16))
        eoF = es3.enter_context(nc.sbuf_tensor("eoF", [128, 2048], BF16))
        eoT = es3.enter_context(nc.sbuf_tensor("eoT", [128, 4 * 1024], F32))

        es3p = ExitStack()
        gwT_s = es3p.enter_context(nc.sbuf_tensor("gwT_s", [128, DC * E], BF16))
        sigS = es3p.enter_context(nc.sbuf_tensor("sigS", [E, NQ], F32))
        R_sb = es3p.enter_context(nc.sbuf_tensor("R_sb", [E, NQ], BF16))
        smsk = es3p.enter_context(nc.sbuf_tensor("smsk", [128, 16], F32))
        rst = es3p.enter_context(nc.sbuf_tensor("rst", [128, 8], F32))
        mskR = es3p.enter_context(nc.sbuf_tensor("mskR", [E, NQ], BF16))
        posF = es3p.enter_context(nc.sbuf_tensor("posF", [E, NQ], F32))
        iot = es3p.enter_context(nc.sbuf_tensor("iot", [128, E], F32))
        msk2 = es3p.enter_context(nc.sbuf_tensor("msk2", [128, 16], F32))
        m1a = es3p.enter_context(nc.sbuf_tensor("m1a", [128, 16 * 16], F32))
        m12a = es3p.enter_context(nc.sbuf_tensor("m12a", [128, 16 * 16], F32))
        rta = es3p.enter_context(nc.sbuf_tensor("rta", [128, 16 * 16], BF16))
        wrk = es3p.enter_context(nc.sbuf_tensor("wrk", [128, 16], F32))
        flt = es3p.enter_context(nc.sbuf_tensor("flt", [128, 32], F32))
        rts = es3p.enter_context(nc.sbuf_tensor("rts", [128, 32], F32))
        xaug = es3p.enter_context(nc.sbuf_tensor("xaug", [128, 4 * 1032], BF16))
        psSc = es3p.enter_context(nc.psum_tensor("psSc", [16, 512], F32))
        psTk = es3p.enter_context(nc.psum_tensor("psTk", [128, 512], F32))
        psTk2 = es3p.enter_context(nc.psum_tensor("psTk2", [128, 512], F32))
        psTkB = es3p.enter_context(nc.psum_tensor("psTkB", [16, 2 * 128], BF16))
        psAh = es3p.enter_context(nc.psum_tensor("psAh", [128, 2048], F32))
        psAh16 = psAh[:].bitcast(BF16)
        fti = es_r.enter_context(nc.sbuf_tensor("fti", [128, 32], mybir.dt.int32,
                                                side="right"))

        ngw = gdma(gwT_s[:].rearrange("p (c j) -> p c j", c=DC),
                   gwT_d.rearrange("(c p) j -> p c j", p=128))
        gdma(e16_b[:], e16_d)
        ne16 = gdma(iot[:], iota_d)

        def preload_w(ex):
            shared = ex == 0
            e = ex - 1
            wb = ex % 2
            srcs = (ws_gate, ws_up, ws_down) if shared else \
                   (w_gate[e], w_up[e], w_down[e])
            gdma(wgu[:, wb * 2 * DC * 512:wb * 2 * DC * 512 + DC * 512].rearrange(
                "p (d j) -> p d j", d=DC),
                srcs[0].rearrange("(d p) j -> p d j", p=128), s=f"ge{wb}")
            gdma(wgu[:, wb * 2 * DC * 512 + DC * 512:(wb + 1) * 2 * DC * 512].rearrange(
                "p (d j) -> p d j", d=DC),
                srcs[1].rearrange("(d p) j -> p d j", p=128), s=f"ge{wb}")
            return gdma(wdn[:, wb * 4 * 1024:(wb + 1) * 4 * 1024].rearrange(
                "p (m j) -> p m j", m=4),
                srcs[2].rearrange("(m p) j -> p m j", p=128), s=f"ge{wb}")
        n_wex_pre = {0: preload_w(0), 1: preload_w(1)}

        cur_pa = [psAh]
        cur_pa16 = [psAh16]
        sh_half = [True]
        t_dn, v_hp_h, s_sl_h, up_h, n_wex = {}, {}, {}, {}, {}
        gu_units = []
        fu = [0]
        pend_dn = []
        v_ra_seq = {}
        gcnt = [0]
        t_xg_h = {}
        dcnt = [0]
        sa_guard = {}
        n_seo = [0]
        v_xg_last = {}
        v_rlx = {}
        n_sxf = [0, 0]

        def unit_guard():
            s = fu[0]; fu[0] += 1
            w = []
            back = 1 if sh_half[0] else 2
            if s - back in v_ra_seq:
                w.append(v_ra_seq[s - back])
            return s, w

        def uslot(s):
            return 0 if sh_half[0] else s % 2

        def emit_down(key):
            ex, u2, slot, s_ = key[:4]
            shared = ex == 0
            wb = ex % 2
            pa, pa16 = key[4], key[5]
            hb = u2 % 2
            whp = [("S", s_sl_h[(ex, u2)]), up_h[(ex, u2)]]
            if len(gu_units) >= 3:
                whp.append(("T", t_dn[gu_units[-3]]))
            v_hp_h[(ex, u2)] = em("V", whp, lambda v, hb=hb: v.tensor_tensor(
                h_p[:, hb * 1024:(hb + 1) * 1024],
                u_p[:, hb * 1024:(hb + 1) * 1024],
                silu_g[:, hb * 1024:(hb + 1) * 1024], ALU.mult), inc="V")
            def dnmm(t, wb=wb, slot=slot, u2=u2, pa=pa):
                r = None
                for hh in range(DC):
                    for mc in range(4):
                        r = t.matmul(
                            pa[:, slot * 2048 + hh * 256:slot * 2048 + (hh + 1) * 256],
                            wdn[:, wb * 4 * 1024 + mc * 1024 + hh * 128:
                                wb * 4 * 1024 + mc * 1024 + (hh + 1) * 128],
                            h_p[:, (u2 % 2) * 1024 + mc * 256:(u2 % 2) * 1024 + (mc + 1) * 256],
                            start=(mc == 0), stop=(mc == 3))
                return r
            t_dn[(ex, u2)] = em("T", [("V", v_hp_h[(ex, u2)])], dnmm, inc="T")
            s_eoF = em("S", [("T", t_dn[(ex, u2)])], lambda s, slot=slot, pa=pa:
                s.activation(eoF[:], pa[:, slot * 2048:(slot + 1) * 2048],
                             AF.Identity), inc="S")
            def eotr(t, slot=slot, pa16=pa16):
                r = None
                for k in range(2):
                    for hh in range(DC):
                        r = t.transpose(
                            pa16[:, slot * 4096 + k * 1024 + hh * 128:
                                 slot * 4096 + k * 1024 + (hh + 1) * 128],
                            eoF[:, hh * 256 + k * 128:hh * 256 + (k + 1) * 128],
                            id_b[:])
                return r
            t_tr = em("T", [("S", s_eoF)], eotr, inc="T")
            dc = dcnt[0]; dcnt[0] += 1
            ep = dc % 2
            weo = [("T", t_tr)]
            if dc - 2 in sa_guard:
                weo.append(sa_guard[dc - 2])
            s_eoT = chain("S", weo, [
                lambda s, slot=slot, ep=ep, pa16=pa16: s.activation(
                    eoT[:, ep * 2048:ep * 2048 + 1024],
                    pa16[:, slot * 4096:slot * 4096 + 1024], AF.Identity),
                lambda s, slot=slot, ep=ep, pa16=pa16: s.activation(
                    eoT[:, ep * 2048 + 1024:ep * 2048 + 2048],
                    pa16[:, slot * 4096 + 1024:slot * 4096 + 2048], AF.Identity),
            ])
            v_ra_seq[s_] = ("S", s_eoT)
            prev_seo = n_seo[0]
            for k in range(2):
                tl = u2 * 2 + k
                if shared:
                    dst_ap = shr_d[1 + tl * 128:1 + (tl + 1) * 128, :]
                else:
                    dst_ap = eoslot[(ex - 1) * CAP + tl * 128:
                                    (ex - 1) * CAP + (tl + 1) * 128, :]
                n_seo[0] = em("Y", [("S", s_eoT), ("seo", prev_seo)],
                    lambda y, ep=ep, k=k, dst_ap=dst_ap: y.dma_start(
                        out=dst_ap,
                        in_=eoT[:, ep * 2048 + k * 1024:ep * 2048 + (k + 1) * 1024]),
                    inc="seo", incby=16)
            sa_guard[dc] = ("seo", n_seo[0])

        def emit_gather(ex, tl):
            e = ex - 1
            wb = ex % 2
            if pend_dn:
                emit_down(pend_dn.pop(0))
            s_, w_ = unit_guard()
            slot = uslot(s_)
            pa, pa16 = cur_pa[0], cur_pa16[0]
            gc = gcnt[0]; gcnt[0] += 1
            gb = gc % 4
            gsem = f"gg{gb}"
            wdma = [("sxp0", n_sxf[0]), ("sxp1", n_sxf[1])]
            if gc - 4 in t_xg_h:
                wdma.append(("T", t_xg_h[gc - 4]))
            ndma = em("G", wdma, lambda g, gb=gb, tl=tl, e=e: g.dma_start(
                out=xgb[:, gb * 1040:gb * 1040 + 1032],
                in_=xslot[e * CAP + tl * 128:e * CAP + (tl + 1) * 128, :]),
                inc=gsem, incby=16)
            def xtr(t, gb=gb, slot=slot, pa16=pa16):
                for d in range(DC):
                    t.transpose(
                        pa16[:, slot * 4096 + d * 128:slot * 4096 + (d + 1) * 128],
                        xgb[:, gb * 1040 + d * 128:gb * 1040 + (d + 1) * 128],
                        id_b[:])
                return t.transpose(
                    pa16[0:1, slot * 4096 + 1024:slot * 4096 + 1152],
                    xgb[:, gb * 1040 + 1024:gb * 1040 + 1025], id_b[:])
            t_xg_h[gc] = em("T", w_ + [(gsem, ndma)], xtr, inc="T")
            v_xg = chain("V", [("T", t_xg_h[gc])], [
                lambda v, wb=wb, tl=tl, slot=slot, pa16=pa16: v.tensor_copy(
                    xgT[:, wb * DC * CAP:(wb + 1) * DC * CAP].rearrange(
                        "p (c n) -> p c n", c=DC)[:, :, tl * 128:(tl + 1) * 128],
                    pa16[:, slot * 4096:slot * 4096 + 1024].rearrange(
                        "p (c n) -> p c n", c=DC)),
                lambda v, wb=wb, tl=tl, slot=slot, pa16=pa16: v.tensor_copy(
                    rowb[0:1, wb * CAP + tl * 128:wb * CAP + (tl + 1) * 128],
                    pa16[0:1, slot * 4096 + 1024:slot * 4096 + 1152]),
            ])
            v_ra_seq[s_] = ("V", v_xg)
            v_xg_last[ex] = v_xg

        def emit_rlx(ex):
            wb = ex % 2
            if pend_dn:
                emit_down(pend_dn.pop(0))
            s_, w_ = unit_guard()
            slot = uslot(s_)
            pa = cur_pa[0]
            w_.append(("V", v_xg_last[ex]))
            t_rl = em("T", w_, lambda t, wb=wb, slot=slot, pa=pa: t.matmul(
                pa[:, slot * 2048:slot * 2048 + CAP],
                ones_b[0:1, 0:128],
                rowb[0:1, wb * CAP:(wb + 1) * CAP], start=True, stop=True), inc="T")
            v_rlx[ex] = em("V", [("T", t_rl)], lambda v, wb=wb, slot=slot, pa=pa:
                v.tensor_copy(rlx[:, wb * CAP:(wb + 1) * CAP],
                              pa[:, slot * 2048:slot * 2048 + CAP]), inc="V")
            v_ra_seq[s_] = ("V", v_rlx[ex])

        def prefetch_for(exn):
            if exn > E:
                return
            for tl in range(CT):
                emit_gather(exn, tl)
            emit_rlx(exn)

        def emit_unit(ex, u2):
            shared = ex == 0
            wb = ex % 2
            if pend_dn:
                emit_down(pend_dn.pop(0))
            s_, w_ = unit_guard()
            slot = uslot(s_)
            pa, pa16 = cur_pa[0], cur_pa16[0]
            hb = u2 % 2
            w_.append((f"ge{wb}", n_wex[ex]))
            if not shared:
                w_.append(("V", v_xg_last[ex]))
                w_.append(("V", v_rlx[ex]))
            def gumm(t, wb=wb, u2=u2, slot=slot, shared=shared, pa=pa):
                r = None
                src_off = (lambda d: d * NQ + u2 * 256) if shared else \
                          (lambda d: wb * DC * CAP + d * CAP + u2 * 256)
                srcT = x2T if shared else xgT
                for mc in range(4):
                    for d in range(DC):
                        t.matmul(
                            pa[:, slot * 2048 + mc * 256:slot * 2048 + (mc + 1) * 256],
                            wgu[:, wb * 2 * DC * 512 + d * 512 + mc * 128:
                                wb * 2 * DC * 512 + d * 512 + (mc + 1) * 128],
                            srcT[:, src_off(d):src_off(d) + 256],
                            start=(d == 0), stop=(d == DC - 1))
                for mc in range(4):
                    for d in range(DC):
                        r = t.matmul(
                            pa[:, slot * 2048 + 1024 + mc * 256:
                               slot * 2048 + 1024 + (mc + 1) * 256],
                            wgu[:, wb * 2 * DC * 512 + DC * 512 + d * 512 + mc * 128:
                                wb * 2 * DC * 512 + DC * 512 + d * 512 + (mc + 1) * 128],
                            srcT[:, src_off(d):src_off(d) + 256],
                            start=(d == 0), stop=(d == DC - 1))
                return r
            t_gu = em("T", w_, gumm, inc="T")
            s_sl_h[(ex, u2)] = em("S", [("T", t_gu)], lambda s, slot=slot, hb=hb, pa=pa:
                s.activation(silu_g[:, hb * 1024:(hb + 1) * 1024],
                             pa[:, slot * 2048:slot * 2048 + 1024], AF.Silu),
                inc="S")
            if shared:
                s_up_ = em("S", [("T", t_gu)], lambda s, slot=slot, hb=hb, pa=pa:
                    s.activation(u_p[:, hb * 1024:(hb + 1) * 1024],
                                 pa[:, slot * 2048 + 1024:slot * 2048 + 2048],
                                 AF.Identity), inc="S")
                up_h[(ex, u2)] = ("S", s_up_)
            else:
                def upb(v, wb=wb, u2=u2, slot=slot, hb=hb, pa=pa):
                    return v.tensor_tensor(
                        u_p[:, hb * 1024:(hb + 1) * 1024].rearrange(
                            "p (m t) -> p m t", m=4),
                        pa[:, slot * 2048 + 1024:slot * 2048 + 2048].rearrange(
                            "p (m t) -> p m t", m=4),
                        rlx[:, wb * CAP + u2 * 256:wb * CAP + (u2 + 1) * 256]
                            .unsqueeze(1).broadcast_to([128, 4, 256]),
                        ALU.mult)
                v_up_ = em("V", [("T", t_gu)], upb, inc="V")
                up_h[(ex, u2)] = ("V", v_up_)
            pend_dn.append((ex, u2, slot, s_, pa, pa16))
            gu_units.append((ex, u2))

        s_scev = {}
        for qc in range(NQC):
            wt = [("gsetup", ne16)]
            if qc - 1 in s_scev:
                wt.append(("S", s_scev[qc - 1]))
            def scmm(t, qc=qc):
                r = None
                for d in range(DC):
                    r = t.matmul(psSc[0:E, :],
                                 gwT_s[:, d * E:(d + 1) * E],
                                 x2T[:, d * NQ + qc * 512:d * NQ + (qc + 1) * 512],
                                 start=(d == 0), stop=(d == DC - 1))
                return r
            t_sc = em("T", wt, scmm, inc="T")
            s_scev[qc] = em("S", [("T", t_sc)], lambda s, qc=qc: s.activation(
                sigS[:, qc * 512:(qc + 1) * 512], psSc[0:E, :],
                AF.Sigmoid), inc="S")
        s_sig_done = s_scev[NQC - 1]

        n_wex[0] = n_wex_pre[0]
        n_wex[1] = n_wex_pre[1]
        shared_q = list(range(8))
        site = [0]

        def pump():
            site[0] += 1
            if shared_q and site[0] % 4 == 0:
                emit_unit(0, shared_q.pop(0))

        tkbank = [psTk, psTk2]

        def tk_ap(g):
            return tkbank[g % 2][:, (g // 2) * 16:(g // 2) * 16 + 16]

        v_r2, v_t2_h, t_tk_h, t_rT_h = {}, {}, {}, {}

        def emit_ttk(g):
            wt = [("S", s_sig_done)]
            if g - 2 in v_t2_h:
                wt.append(("V", v_t2_h[g - 2]))
            t_tk_h[g] = em("T", wt, lambda t, g=g: t.transpose(
                tk_ap(g), sigS[0:E, g * 128:(g + 1) * 128],
                id_f[0:16, 0:16]), inc="T")

        def emit_rt(g):
            wrt = [("V", v_t2_h[g])]
            if g - 1 in v_r2:
                wrt.append(("V", v_r2[g - 1]))
            buf2 = g % 2
            t_rT_h[g] = em("T", wrt, lambda t, buf2=buf2, g=g: t.transpose(
                psTkB[0:16, buf2 * 128:(buf2 + 1) * 128],
                rta[:, g * 16:(g + 1) * 16], id_b[:]), inc="T")
            v_r2[g] = em("V", [("T", t_rT_h[g])], lambda v, g=g, buf2=buf2:
                v.tensor_copy(R_sb[:, g * 128:(g + 1) * 128],
                              psTkB[0:16, buf2 * 128:(buf2 + 1) * 128]), inc="V")

        emit_ttk(0)
        emit_ttk(1)
        for g in range(16):
            pg = tk_ap(g)
            v_t2_h[g] = chain("V", [("T", t_tk_h[g])], [
                lambda v, pg=pg: v.tensor_reduce(rst[:, 0:1], pg, AX.X, ALU.max),
                lambda v, g=g, pg=pg: v.tensor_scalar(m1a[:, g * 16:(g + 1) * 16],
                                          pg, rst[:, 0:1],
                                          None, ALU.is_ge),
                lambda v, g=g, pg=pg: v.scalar_tensor_tensor(smsk[:],
                                                 m1a[:, g * 16:(g + 1) * 16], -1e30,
                                                 pg, ALU.mult, ALU.add),
                lambda v: v.tensor_reduce(rst[:, 1:2], smsk[:], AX.X, ALU.max),
                lambda v, g=g, pg=pg: v.tensor_scalar(m12a[:, g * 16:(g + 1) * 16],
                                          pg, rst[:, 1:2],
                                          None, ALU.is_ge),
                lambda v: v.tensor_tensor(rst[:, 2:3], rst[:, 0:1], rst[:, 1:2],
                                          ALU.add),
                lambda v: v.tensor_scalar(rst[:, 3:4], rst[:, 2:3], 1.0, 1e-20,
                                          ALU.mult, ALU.add),
                lambda v: v.reciprocal(rst[:, 4:5], rst[:, 3:4]),
                lambda v: v.tensor_scalar(rst[:, 5:6], rst[:, 4:5], 2.5, None,
                                          ALU.mult),
                lambda v, g=g, pg=pg: v.scalar_tensor_tensor(rta[:, g * 16:(g + 1) * 16],
                                                 pg, rst[:, 5:6],
                                                 m12a[:, g * 16:(g + 1) * 16],
                                                 ALU.mult, ALU.mult),
            ])
            if g + 2 < 16:
                emit_ttk(g + 2)
            if g - 2 >= 0:
                emit_rt(g - 2)
            pump()
        emit_rt(14)
        emit_rt(15)
        v_route_done = v_r2[15]

        v_msk = em("V", [("V", v_route_done)], lambda v: v.tensor_scalar(
            mskR[:], R_sb[:], 0.0, None, ALU.is_gt), inc="V")
        v_pos = em("V", [("V", v_msk)], lambda v: v.tensor_tensor_scan(
            posF[:], mskR[:], mskR[:], -1.0, ALU.add, ALU.bypass), inc="V")

        v_flt, t_ps_h = {}, {}
        n_ld_h, n_sc_h = {}, {}
        em("V", [], lambda v: v.memset(
            xaug[:].rearrange("p (k c) -> p k c", k=4)[:, :, 1025:1032], 0.0),
            inc="V")

        def emit_sc(g):
            xs = f"sxp{g % 2}"
            for k in range(2):
                sl = (g % 2) * 2 + k
                wrc = [("V", v_flt[g])]
                if g - 2 in n_sc_h:
                    wrc.append((xs, n_sc_h[g - 2]))
                v_rc = em("V", wrc, lambda v, g=g, sl=sl, k=k:
                    v.tensor_copy(xaug[:, sl * 1032 + 1024:sl * 1032 + 1025],
                                  rts[:, 2 * g + k:2 * g + k + 1]), inc="V")
                wsc = [("V", v_rc), (f"xap{g % 2}", n_ld_h[g])]
                if g == 0 and k == 0:
                    wsc.append(("gz", n_gz))
                n_sc_h[g] = em("G", wsc, lambda gg, g=g, k=k, sl=sl:
                    gg.indirect_dma_start(
                        out=xslot,
                        out_offset=IndirectOffsetOnAxis(
                            fti[:, 2 * g + k:2 * g + k + 1], 0),
                        in_=xaug[:, sl * 1032:(sl + 1) * 1032], in_offset=None),
                    inc=xs, incby=16)

        def emit_tps(g):
            wt = [("V", v_pos)]
            if g - 2 in v_flt:
                wt.append(("V", v_flt[g - 2]))
            t_ps_h[g] = em("T", wt, lambda t, g=g: t.transpose(
                tk_ap(g), posF[0:E, g * 128:(g + 1) * 128],
                id_f[0:16, 0:16]), inc="T")

        emit_tps(0)
        emit_tps(1)
        for g in range(16):
            pg = tk_ap(g)
            v_flt[g] = chain("V", [("T", t_ps_h[g]), ("gsetup", ne16)], [
                lambda v, g=g: v.tensor_tensor(msk2[:], m12a[:, g * 16:(g + 1) * 16],
                                               m1a[:, g * 16:(g + 1) * 16],
                                               ALU.subtract),
                lambda v, g=g: v.tensor_tensor(wrk[:], m1a[:, g * 16:(g + 1) * 16],
                                               iot[:], ALU.mult),
                lambda v: v.tensor_reduce(rst[:, 0:1], wrk[:], AX.X, ALU.add),
                lambda v, g=g, pg=pg: v.tensor_tensor(wrk[:], m1a[:, g * 16:(g + 1) * 16],
                                               pg, ALU.mult),
                lambda v: v.tensor_reduce(rst[:, 1:2], wrk[:], AX.X, ALU.add),
                lambda v: v.tensor_scalar(rst[:, 1:2], rst[:, 1:2],
                                          float(CAP - 1), None, ALU.min),
                lambda v, g=g: v.scalar_tensor_tensor(
                    flt[:, 2 * g:2 * g + 1], rst[:, 0:1], float(CAP),
                    rst[:, 1:2], ALU.mult, ALU.add),
                lambda v, g=g: v.tensor_tensor(wrk[:], m1a[:, g * 16:(g + 1) * 16],
                                               rta[:, g * 16:(g + 1) * 16], ALU.mult),
                lambda v, g=g: v.tensor_reduce(rts[:, 2 * g:2 * g + 1], wrk[:],
                                               AX.X, ALU.add),
                lambda v, g=g: v.tensor_tensor(wrk[:], msk2[:], iot[:], ALU.mult),
                lambda v: v.tensor_reduce(rst[:, 2:3], wrk[:], AX.X, ALU.add),
                lambda v, g=g, pg=pg: v.tensor_tensor(wrk[:], msk2[:], pg,
                                               ALU.mult),
                lambda v: v.tensor_reduce(rst[:, 3:4], wrk[:], AX.X, ALU.add),
                lambda v: v.tensor_scalar(rst[:, 3:4], rst[:, 3:4],
                                          float(CAP - 1), None, ALU.min),
                lambda v, g=g: v.scalar_tensor_tensor(
                    flt[:, 2 * g + 1:2 * g + 2], rst[:, 2:3], float(CAP),
                    rst[:, 3:4], ALU.mult, ALU.add),
                lambda v, g=g: v.tensor_tensor(wrk[:], msk2[:],
                                               rta[:, g * 16:(g + 1) * 16], ALU.mult),
                lambda v, g=g: v.tensor_reduce(rts[:, 2 * g + 1:2 * g + 2], wrk[:],
                                               AX.X, ALU.add),
                lambda v, g=g: v.tensor_copy(fti[:, 2 * g:2 * g + 2],
                                             flt[:, 2 * g:2 * g + 2]),
            ])
            if g + 2 < 16:
                emit_tps(g + 2)
            pump()
            for k in range(2):
                sl = (g % 2) * 2 + k
                wl = []
                if k == 0 and g - 2 in n_sc_h:
                    wl.append((f"sxp{g % 2}", n_sc_h[g - 2]))
                n_ld_h[g] = gdma(
                    xaug[:, sl * 1032:sl * 1032 + 1024],
                    x2_scr[1 + g * 128:1 + (g + 1) * 128, :], waits=wl,
                    s=f"xap{g % 2}")
            if g - 1 >= 0:
                emit_sc(g - 1)
        emit_sc(15)
        while shared_q:
            emit_unit(0, shared_q.pop(0))
        while pend_dn:
            emit_down(pend_dn.pop(0))
        n_sxf[0], n_sxf[1] = n_sc_h[14], n_sc_h[15]
        play()
        es3p.close()

        psA = es3.enter_context(nc.psum_tensor("psA", [128, 4096], F32))
        psA16 = psA[:].bitcast(BF16)
        cur_pa[0] = psA
        cur_pa16[0] = psA16
        sh_half[0] = False

        prefetch_for(1)
        for ex in range(1, E + 1):
            e = ex - 1
            wb = ex % 2
            if ex >= 2:
                wwait = [("T", t_dn[(ex - 2, (8 if ex - 2 == 0 else CAP // 256) - 1)])]
                srcs = (w_gate[e], w_up[e], w_down[e])
                gdma(wgu[:, wb * 2 * DC * 512:wb * 2 * DC * 512 + DC * 512].rearrange(
                    "p (d j) -> p d j", d=DC),
                    srcs[0].rearrange("(d p) j -> p d j", p=128), waits=wwait,
                    s=f"ge{wb}")
                gdma(wgu[:, wb * 2 * DC * 512 + DC * 512:(wb + 1) * 2 * DC * 512].rearrange(
                    "p (d j) -> p d j", d=DC),
                    srcs[1].rearrange("(d p) j -> p d j", p=128), s=f"ge{wb}")
                n_wex[ex] = gdma(wdn[:, wb * 4 * 1024:(wb + 1) * 4 * 1024].rearrange(
                    "p (m j) -> p m j", m=4),
                    srcs[2].rearrange("(m p) j -> p m j", p=128), s=f"ge{wb}")
            for u2 in range(CAP // 256):
                emit_unit(ex, u2)
                if u2 == 0 and ex < E:
                    prefetch_for(ex + 1)
        while pend_dn:
            emit_down(pend_dn.pop(0))
        em("G", [("ge0", ct.get("ge0", 0)), ("ge1", ct.get("ge1", 0)),
                 ("gg0", ct.get("gg0", 0)), ("gg1", ct.get("gg1", 0)),
                 ("gg2", ct.get("gg2", 0)), ("gg3", ct.get("gg3", 0)),
                 ("sxp0", ct.get("sxp0", 0)), ("sxp1", ct.get("sxp1", 0))],
           lambda g: None)
        em("Y", [("seo", ct.get("seo", 0))], lambda y: None)
        play()
        es3.close()

        es4 = ExitStack()
        xrs = es4.enter_context(nc.sbuf_tensor("xrs", [128, 4 * D], F32))
        sqj2 = es4.enter_context(nc.sbuf_tensor("sqj2", [128, D], BF16))
        st2 = es4.enter_context(nc.sbuf_tensor("st2", [128, 16], F32))
        ytl = es4.enter_context(nc.sbuf_tensor("ytl", [128, 2 * D], F32))
        x2i_s = es4.enter_context(nc.sbuf_tensor("x2i_s", [128, 16], mybir.dt.int32))
        nx2i = gdma(x2i_s[:], x2i_d)

        v_fin, ny_h = {}, {}
        seo_tot = ct.get("seo", 0)
        for g in range(16):
            buf = g % 4
            dest = xrs[:, buf * D:(buf + 1) * D]
            wx = [("V", v_fin[g - 4])] if g - 4 in v_fin else []
            if g == 0:
                wx.append(("seo", seo_tot))
            nshr = ydma(dest, shr_d[1 + g * 128:1 + (g + 1) * 128, :],
                        waits=wx, s=f"yr{buf % 2}")
            wa1 = [(f"yr{buf % 2}", nshr), ("gsetup", nx2i)]
            if g == 0:
                wa1.append(("seo", seo_tot))
            a1 = em("G", wa1, lambda gg, g=g, dest=dest: gg.indirect_dma_start(
                out=dest, out_offset=None, in_=x2_scr,
                in_offset=IndirectOffsetOnAxis(x2i_s[:, g:g + 1], 0),
                compute_op=ALU.add), inc=f"gg{buf}", incby=16)
            a2 = em("G", [(f"gg{buf}", a1)], lambda gg, g=g, dest=dest:
                gg.indirect_dma_start(
                    out=dest, out_offset=None, in_=eoslot,
                    in_offset=IndirectOffsetOnAxis(fti[:, 2 * g:2 * g + 1], 0),
                    compute_op=ALU.add), inc=f"gg{buf}", incby=16)
            a3 = em("G", [(f"gg{buf}", a2)], lambda gg, g=g, dest=dest:
                gg.indirect_dma_start(
                    out=dest, out_offset=None, in_=eoslot,
                    in_offset=IndirectOffsetOnAxis(fti[:, 2 * g + 1:2 * g + 2], 0),
                    compute_op=ALU.add), inc=f"gg{buf}", incby=16)
            v_s1 = chain("V", [(f"gg{buf}", a3)], [
                lambda v, dest=dest: v.tensor_reduce(
                    st2[:, 0:1], dest, AX.X, ALU.add, negate=True),
                lambda v, dest=dest: v.scalar_tensor_tensor(
                    sqj2[:], dest, 1.0, dest,
                    ALU.mult, ALU.mult, accum_out=st2[:, 2:3]),
                lambda v: v.tensor_scalar(st2[:, 1:2], st2[:, 0:1], 1.0 / D,
                                          None, ALU.mult),
                lambda v: v.tensor_scalar(st2[:, 3:4], st2[:, 2:3], 1.0 / D,
                                          None, ALU.mult),
                lambda v: v.tensor_tensor(st2[:, 4:5], st2[:, 1:2], st2[:, 1:2],
                                          ALU.mult),
                lambda v: v.tensor_tensor(st2[:, 5:6], st2[:, 3:4], st2[:, 4:5],
                                          ALU.subtract),
                lambda v: v.tensor_scalar(st2[:, 6:7], st2[:, 5:6], 1.0, EPS_LN,
                                          ALU.mult, ALU.add),
            ])
            s_rs2 = chain("S", [("V", v_s1)], [
                lambda s: s.activation(st2[:, 7:8], st2[:, 6:7], AF.Ln),
                lambda s: s.activation(st2[:, 8:9], st2[:, 7:8], AF.Exp, scale=-0.5),
            ])
            wyb = [("S", s_rs2)]
            if g - 2 in ny_h:
                wyb.append((f"yw{g % 2}", ny_h[g - 2]))
            v_fin[g] = chain("V", wyb, [
                lambda v: v.tensor_tensor(st2[:, 9:10], st2[:, 1:2], st2[:, 8:9],
                                          ALU.mult),
                lambda v, g=g, dest=dest: v.scalar_tensor_tensor(
                    ytl[:, (g % 2) * D:(g % 2 + 1) * D], dest,
                    st2[:, 8:9],
                    st2[:, 9:10].broadcast_to([128, D]), ALU.mult, ALU.add),
            ])
            ny_h[g] = ydma(y_out[g * 128:(g + 1) * 128, :],
                           ytl[:, (g % 2) * D:(g % 2 + 1) * D],
                           waits=[("V", v_fin[g])], s=f"yw{g % 2}")
        em("Y", [("yw0", ct["yw0"]), ("yw1", ct["yw1"])], lambda y: None)
        play()
        es4.close()
        es_r.close()

    nc.finalize()
    return nc


def make_in_maps(inputs):
    f32 = np.float32
    x = np.asarray(inputs["L0_attn_x"], f32)
    Wq = np.asarray(inputs["L0_attn_Wq"], f32)
    bq = np.asarray(inputs["L0_attn_bq"], f32)
    Wk = np.asarray(inputs["L0_attn_Wk"], f32)
    bk = np.asarray(inputs["L0_attn_bk"], f32)
    Wv = np.asarray(inputs["L0_attn_Wv"], f32)
    bv = np.asarray(inputs["L0_attn_bv"], f32)
    Wo = np.asarray(inputs["L0_attn_Wo"], f32)
    bo = np.asarray(inputs["L0_attn_bo"], f32)
    qn = np.asarray(inputs["L0_attn_qn_w"], f32)
    kn = np.asarray(inputs["L0_attn_kn_w"], f32)
    gate_w = np.asarray(inputs["L0_ffn_gate_w"], f32)

    def _ne(w):
        a = np.zeros((2, 128), f32)
        a[0, :64] = w
        a[1, 64:] = w
        return a
    e2a = np.zeros((1, 128), f32); e2a[0, :64] = 1.0
    e2b = np.zeros((1, 128), f32); e2b[0, 64:] = 1.0
    eg = np.zeros((H, NP * 128), f32)
    for c in range(NP):
        eg[2 * c, c * 128:c * 128 + 64] = 1.0
        eg[2 * c + 1, c * 128 + 64:(c + 1) * 128] = 1.0
    h2 = np.zeros((128, 2), f32)
    h2[:64, 0] = 1.0
    h2[64:, 1] = 1.0
    e16 = np.zeros((E, E * 128), f32)
    for e in range(E):
        e16[e, e * 128:(e + 1) * 128] = 1.0

    common = dict(
        Wq=Wq, Wk=Wk, Wv=Wv, Wo=Wo,
        bq_c=np.ascontiguousarray(bq[:D].reshape(DC, 128).T),
        bgate_c=np.ascontiguousarray(bq[D:, None]),
        bk_c=np.ascontiguousarray(bk.reshape(DC, 128).T),
        bv_r=np.ascontiguousarray(bv[None, :]),
        bo_c=np.ascontiguousarray(bo.reshape(DC, 128).T),
        qn_r=np.ascontiguousarray(qn[None, :]),
        kn_r=np.ascontiguousarray(kn[None, :]),
        qne_h=_ne(qn), kne_h=_ne(kn),
        gwT=np.ascontiguousarray(gate_w.T),
        w_gate=np.asarray(inputs["L0_ffn_w_gate"], f32),
        w_up=np.asarray(inputs["L0_ffn_w_up"], f32),
        w_down=np.asarray(inputs["L0_ffn_w_down"], f32),
        ws_gate=np.asarray(inputs["L0_ffn_ws_gate"], f32),
        ws_up=np.asarray(inputs["L0_ffn_ws_up"], f32),
        ws_down=np.asarray(inputs["L0_ffn_ws_down"], f32),
        ident=np.eye(128, dtype=f32),
        ones_r=np.ones((1, 512), f32),
        e2a=e2a, e2b=e2b, eg=eg, e16=e16, h2=h2,
        iota16c=np.tile(np.arange(E, dtype=f32)[None, :], (128, 1)),
        x2i=(1 + np.arange(16, dtype=np.int32)[None, :] * 128
             + np.arange(128, dtype=np.int32)[:, None]),
    )
    in_maps = []
    for c in range(8):
        b, q0 = c // 2, (c % 2) * NQ
        xb = x[b]
        r0 = (q0 + NQ) % NKV
        xkv_c = np.ascontiguousarray(np.concatenate([xb[q0:q0 + NQ], xb[r0:r0 + NQ]]))
        m = dict(common)
        m["xkv"] = xkv_c
        in_maps.append(m)
    return in_maps


def kernel(**inputs):
    from concourse.bass_utils import run_bass_kernel_spmd
    nc = build_nc()
    in_maps = make_in_maps(inputs)
    res = run_bass_kernel_spmd(nc, in_maps, core_ids=list(range(8)))
    y = np.empty((4, NKV, D), np.float32)
    for c in range(8):
        b, q0 = c // 2, (c % 2) * NQ
        y[b, q0:q0 + NQ] = np.asarray(res.results[c]["y"], np.float32)
    return y




# version 25
# speedup vs baseline: 1.3081x; 1.3081x over previous
import numpy as np
from contextlib import ExitStack
import concourse.bass as bass
import concourse.mybir as mybir
import concourse.library_config as libcfg
from concourse.bass_core_rust import IndirectOffsetOnAxis

F32 = mybir.dt.float32
BF16 = mybir.dt.bfloat16
AF = mybir.ActivationFunctionType
ALU = mybir.AluOpType
AX = mybir.AxisListType

D, H, HD = 1024, 16, 64
NKV, NQ = 4096, 2048
DC = D // 128
KT = NKV // 128
NP = 8
M, E = 512, 16
EPS_RMS, EPS_LN = 1e-6, 1e-5
PSLOT = 5
NQC = NQ // 512


def build_nc():
    nc = bass.Bass("TRN2", target_bir_lowering=False)

    def I(name, shape):
        return nc.dram_tensor(name, list(shape), F32, kind="ExternalInput").ap()

    xkv = I("xkv", (NKV, D))
    Wq = I("Wq", (D, D + H))
    Wk = I("Wk", (D, D))
    Wv = I("Wv", (D, D))
    Wo = I("Wo", (D, D))
    bq_c = I("bq_c", (128, DC))
    bgate_c = I("bgate_c", (H, 1))
    bk_c = I("bk_c", (128, DC))
    bv_r = I("bv_r", (1, D))
    bo_c = I("bo_c", (128, DC))
    qn_r = I("qn_r", (1, HD))
    kn_r = I("kn_r", (1, HD))
    qne_h = I("qne_h", (2, 128))
    kne_h = I("kne_h", (2, 128))
    gwT_d = I("gwT", (D, E))
    w_gate = I("w_gate", (E, D, M))
    w_up = I("w_up", (E, D, M))
    w_down = I("w_down", (E, M, D))
    ws_gate = I("ws_gate", (D, M))
    ws_up = I("ws_up", (D, M))
    ws_down = I("ws_down", (M, D))
    ident = I("ident", (128, 128))
    ones_r = I("ones_r", (1, 512))
    e2a_d = I("e2a", (1, 128))
    e2b_d = I("e2b", (1, 128))
    eg_d = I("eg", (H, NP * 128))
    e16_d = I("e16", (E, E * 128))
    iota_d = I("iota16c", (128, E))
    h2_d = I("h2", (128, 2))
    x2i_d = nc.dram_tensor("x2i", [128, 16], mybir.dt.int32,
                           kind="ExternalInput").ap()

    y_out = nc.dram_tensor("y", [NQ, D], F32, kind="ExternalOutput").ap()
    x2_scr = nc.dram_tensor("x2_scr", [NQ + 1, D], F32, kind="Internal").ap()
    shr_d = nc.dram_tensor("shr_d", [NQ + 1, D], F32, kind="Internal").ap()
    xslot = nc.dram_tensor("xslot", [16 * 512, 1032], BF16, kind="Internal").ap()
    eoslot = nc.dram_tensor("eoslot", [16 * 512, 1024], F32, kind="Internal").ap()

    ct = {}
    def bump(s, by=1):
        ct[s] = ct.get(s, 0) + by
        return ct[s]

    ops = {"T": [], "V": [], "S": [], "G": [], "Y": []}
    sem = {}

    def em(eng, waits, body, inc=None, incby=1):
        val = bump(inc, incby) if inc else None
        def run(e, waits=tuple(waits), body=body, inc=inc, incby=incby):
            for s_, v_ in waits:
                if v_ is not None and v_ > 0:
                    e.wait_ge(sem[s_], v_)
            last = body(e)
            if inc is not None:
                assert last is not None
                last.then_inc(sem[inc], incby)
        ops[eng].append(run)
        return val

    def chain(eng, waits, steps):
        last = None
        for i, fn in enumerate(steps):
            w = (list(waits) + [(eng, ct[eng])]) if i == 0 else [(eng, ct[eng])]
            last = em(eng, w, fn, inc=eng)
        return last

    def play():
        o = {k: list(v) for k, v in ops.items()}
        for k in ops:
            ops[k] = []
        with nc.Block(no_gpsimd_drain=True) as blk:
            if o["G"]:
                blk.gpsimd(lambda g: [f(g) for f in o["G"]])
            if o["Y"]:
                blk.sync(lambda y: [f(y) for f in o["Y"]])
            if o["T"]:
                blk.tensor(lambda t: [f(t) for f in o["T"]])
            if o["S"]:
                blk.scalar(lambda s: [f(s) for f in o["S"]])
            if o["V"]:
                blk.vector(lambda v: [f(v) for f in o["V"]])

    def gdma(out_ap, in_ap, waits=(), s="gsetup"):
        return em("G", waits, lambda g, o=out_ap, i=in_ap: g.dma_start(out=o, in_=i),
                  inc=s, incby=16)

    def ydma(out_ap, in_ap, waits=(), s=None):
        return em("Y", waits, lambda y, o=out_ap, i=in_ap: y.dma_start(out=o, in_=i),
                  inc=s, incby=16)

    es = ExitStack()
    with es:
        for n in ["gsetup", "gdx0", "gdx1", "gdx2", "gdx3", "gw0", "gw1", "ge0", "ge1",
                  "xr0", "xr1", "x2w", "yr0", "yr1", "yr2", "yr3", "yr4", "yr5",
                  "yr6", "yr7", "yw0", "yw1",
                  "sxp0", "sxp1", "gg0", "gg1", "gg2", "gg3", "gg4", "gg5",
                  "gg6", "gg7", "seo", "xap0", "xap1", "gz",
                  "T", "V", "S"]:
            sem[n] = es.enter_context(nc.semaphore(n))

        id_f = es.enter_context(nc.sbuf_tensor("id_f", [128, 128], F32))
        id_b = es.enter_context(nc.sbuf_tensor("id_b", [128, 128], BF16))
        ones_b = es.enter_context(nc.sbuf_tensor("ones_b", [1, 512], BF16))
        e2a_b = es.enter_context(nc.sbuf_tensor("e2a_b", [1, 128], BF16))
        e2b_b = es.enter_context(nc.sbuf_tensor("e2b_b", [1, 128], BF16))
        eg_b = es.enter_context(nc.sbuf_tensor("eg_b", [H, NP * 128], BF16))
        h2_b = es.enter_context(nc.sbuf_tensor("h2_b", [128, 2], BF16))
        bq_cs = es.enter_context(nc.sbuf_tensor("bq_cs", [128, DC], F32))
        bk_cs = es.enter_context(nc.sbuf_tensor("bk_cs", [128, DC], F32))
        bv_b = es.enter_context(nc.sbuf_tensor("bv_b", [1, D], BF16))
        bo_cs = es.enter_context(nc.sbuf_tensor("bo_cs", [128, DC], F32))
        bgate_s = es.enter_context(nc.sbuf_tensor("bgate_s", [H, 1], F32))
        qn_row = es.enter_context(nc.sbuf_tensor("qn_row", [1, HD], F32))
        kn_row = es.enter_context(nc.sbuf_tensor("kn_row", [1, HD], F32))
        qne_b = es.enter_context(nc.sbuf_tensor("qne_b", [2, 128], BF16))
        kne_b = es.enter_context(nc.sbuf_tensor("kne_b", [2, 128], BF16))
        tiny = es.enter_context(nc.sbuf_tensor("tiny", [1, 8], F32))
        tinyb = es.enter_context(nc.sbuf_tensor("tinyb", [1, 8], BF16))
        es_og = ExitStack()
        OgT = es_og.enter_context(nc.sbuf_tensor("OgT", [128, DC * NQ], BF16))

        es01 = ExitStack()
        xT = es01.enter_context(nc.sbuf_tensor("xT", [128, DC * NKV], BF16))
        Wg_s = es01.enter_context(nc.sbuf_tensor("Wg_s", [128, DC * H], BF16))
        Wq_p = es01.enter_context(nc.sbuf_tensor("Wq_p", [128, 2 * DC * 128], BF16))
        Wk_p = es01.enter_context(nc.sbuf_tensor("Wk_p", [128, 2 * DC * 128], BF16))
        Wv_p = es01.enter_context(nc.sbuf_tensor("Wv_p", [128, 2 * DC * 128], BF16))
        xtile = es01.enter_context(nc.sbuf_tensor("xtile", [128, 4 * D], BF16))
        K_sb = es01.enter_context(nc.sbuf_tensor("K_sb", [128, 2 * NKV], BF16))
        Q_sb = es01.enter_context(nc.sbuf_tensor("Q_sb", [128, 2 * NQ], BF16))
        Vp = es01.enter_context(nc.sbuf_tensor("Vp", [128, 2 * KT * 130], BF16))
        Pring = es01.enter_context(nc.sbuf_tensor("Pring", [128, PSLOT * 1024], BF16))
        Osb = es01.enter_context(nc.sbuf_tensor("Osb", [128, NQ], BF16))
        se_fl = es01.enter_context(nc.sbuf_tensor("se_fl", [1, 2 * NQ], BF16))
        Gden = es01.enter_context(nc.sbuf_tensor("Gden", [128, 2 * 512], F32))
        GdenR = es01.enter_context(nc.sbuf_tensor("GdenR", [128, 2 * 512], BF16))
        Gq_sb = es01.enter_context(nc.sbuf_tensor("Gq_sb", [128, 2 * 512], BF16))
        sgate = es01.enter_context(nc.sbuf_tensor("sgate", [H, NQ], BF16))
        ecol = es01.enter_context(nc.sbuf_tensor("ecol", [128, 1], F32))
        ksq = es01.enter_context(nc.sbuf_tensor("ksq", [128, 512], BF16))
        kb_sb = es01.enter_context(nc.sbuf_tensor("kb_sb", [128, 512], BF16))
        ss1 = es01.enter_context(nc.sbuf_tensor("ss1", [2, 512], F32))
        ss2 = es01.enter_context(nc.sbuf_tensor("ss2", [2, 512], F32))
        rrow = es01.enter_context(nc.sbuf_tensor("rrow", [2, 512], BF16))
        rexp = es01.enter_context(nc.sbuf_tensor("rexp", [128, 512], BF16))

        psP = es01.enter_context(nc.psum_tensor("psP", [128, 2 * 512], F32))
        psS = es01.enter_context(nc.psum_tensor("psS", [128, 2 * 1024], F32))
        psO = es01.enter_context(nc.psum_tensor("psO", [128, 2 * 512], F32))
        psX = psS[:].bitcast(BF16)

        gdma(id_f[:], ident)
        gdma(id_b[:], ident)
        gdma(ones_b[:], ones_r)
        gdma(e2a_b[:], e2a_d)
        gdma(e2b_b[:], e2b_d)
        gdma(eg_b[:], eg_d)
        gdma(h2_b[:], h2_d)
        gdma(bq_cs[:], bq_c)
        gdma(bk_cs[:], bk_c)
        gdma(bv_b[:], bv_r)
        gdma(bo_cs[:], bo_c)
        gdma(bgate_s[:], bgate_c)
        gdma(qn_row[:], qn_r)
        gdma(kn_row[:], kn_r)
        gdma(qne_b[:], qne_h)
        gdma(kne_b[:], kne_h)
        gdma(Wg_s[:].rearrange("p (c j) -> p c j", c=DC),
             Wq.rearrange("(c p) j -> p c j", p=128)[:, :, D:D + H])
        n_setup = ct["gsetup"]
        zzx = OgT[:, 0:1032]
        v_zzx = em("V", [], lambda v: v.memset(zzx, 0.0), inc="V")
        for zi in range(64):
            ydma(xslot[zi * 128:(zi + 1) * 128, :], zzx,
                 waits=[("V", v_zzx)] if zi == 0 else [], s="gz")
        n_gz = ct["gz"]

        v_bias = chain("V", [("gsetup", n_setup)], [
            lambda v: v.tensor_reduce(tiny[:, 0:1], qn_row[:], AX.X, ALU.max,
                                      apply_absolute_value=True),
            lambda v: v.tensor_reduce(tiny[:, 1:2], kn_row[:], AX.X, ALU.max,
                                      apply_absolute_value=True),
            lambda v: v.tensor_tensor(tiny[:, 2:3], tiny[:, 0:1], tiny[:, 1:2], ALU.mult),
            lambda v: v.tensor_scalar(tiny[:, 3:4], tiny[:, 2:3], -8.0, -2.0,
                                      ALU.mult, ALU.add),
            lambda v: v.tensor_copy(tinyb[:, 0:1], tiny[:, 3:4]),
        ])
        t_bias = em("T", [("V", v_bias)], lambda t: t.matmul(
            psO[:, 0:1], ones_b[0:1, 0:128], tinyb[0:1, 0:1], start=True, stop=True),
            inc="T")
        em("V", [("T", t_bias)], lambda v: v.tensor_copy(ecol[:], psO[:, 0:1]), inc="V")
        em("V", [], lambda v: v.memset(
            Vp[:].rearrange("p (t two c) -> p t two c", two=2, c=65)[:, :, :, 64:65], 1.0),
            inc="V")

        t_xtr, v_xev = {}, {}
        for i in range(KT):
            buf = i % 4
            w = [("T", t_xtr[i - 4])] if i - 4 in t_xtr else []
            nld = gdma(xtile[:, buf * D:(buf + 1) * D],
                       xkv[i * 128:(i + 1) * 128, :], waits=w, s=f"gdx{buf}")
            def xt_body(t, buf=buf):
                r = None
                for d in range(DC):
                    r = t.transpose(psX[:, (buf % 2) * D + d * 128:(buf % 2) * D + (d + 1) * 128],
                                    xtile[:, buf * D + d * 128:buf * D + (d + 1) * 128],
                                    id_b[:])
                return r
            wt = [(f"gdx{buf}", nld)]
            if i - 2 in v_xev:
                wt.append(("V", v_xev[i - 2]))
            t_xtr[i] = em("T", wt, xt_body, inc="T")
            v_xev[i] = em("V", [("T", t_xtr[i])], lambda v, i=i, buf=buf: v.tensor_copy(
                xT[:].rearrange("p (c n) -> p c n", c=DC)[:, :, i * 128:(i + 1) * 128],
                psX[:, (buf % 2) * D:((buf % 2) + 1) * D].rearrange("p (c n) -> p c n", c=DC)),
                inc="V")
        v_x_done = v_xev[KT - 1]

        gate_tiles = [psP[0:H, 0:512], psP[0:H, 512:1024],
                      psO[0:H, 0:512], psO[0:H, 512:1024]]
        for qc in range(NQC):
            def g_body(t, qc=qc):
                r = None
                for d in range(DC):
                    r = t.matmul(gate_tiles[qc],
                                 Wg_s[:, d * H:(d + 1) * H],
                                 xT[:, d * NKV + qc * 512:d * NKV + qc * 512 + 512],
                                 start=(d == 0), stop=(d == DC - 1))
                return r
            t_gate = em("T", [("V", v_x_done)], g_body, inc="T")
            em("S", [("T", t_gate)], lambda s, qc=qc: s.activation(
                sgate[:, qc * 512:(qc + 1) * 512], gate_tiles[qc],
                AF.Sigmoid, bias=bgate_s[:]), inc="S")

        play()

        proj_done = {}
        proj_seq = [0]
        proj_T_end = {}
        t_pv_g = {}
        s_exp_g = {}
        gidx = [0]
        nW = {}

        PSL = [psP[:, 0:512], psP[:, 512:1024]]

        def proj_units(c):
            wb = c % 2
            wsem = f"gw{wb}"
            n_w = nW[c]
            woff = wb * DC * 128
            koff = wb * NKV
            qoff = wb * NQ
            voff = wb * KT * 130
            for kind in ("k", "q"):
                if kind == "k":
                    W_s, dst, dof, tcs, bcol, ne = Wk_p, K_sb, koff, NKV // 512, bk_cs, kne_b
                else:
                    W_s, dst, dof, tcs, bcol, ne = Wq_p, Q_sb, qoff, NQ // 512, bq_cs, qne_b
                for tc in range(tcs):
                    seq = proj_seq[0]; proj_seq[0] += 1
                    X = PSL[seq % 2]
                    Y = X
                    wt = [(wsem, n_w)]
                    if seq - 2 in proj_done:
                        wt.append(("V", proj_done[seq - 2]))
                    def kmm(t, X=X, woff=woff, tc=tc, W_s=W_s):
                        r = None
                        for d in range(DC):
                            r = t.matmul(X,
                                         W_s[:, woff + d * 128:woff + (d + 1) * 128],
                                         xT[:, d * NKV + tc * 512:d * NKV + tc * 512 + 512],
                                         start=(d == 0), stop=(d == DC - 1))
                        return r
                    t_mm = em("T", wt, kmm, inc="T")
                    v_kb = chain("V", [("T", t_mm)], [
                        lambda v, X=X, bcol=bcol, c=c: v.tensor_scalar(
                            kb_sb[:, 0:512], X, bcol[:, c:c + 1], None, ALU.add),
                        lambda v: v.tensor_tensor(ksq[:, 0:512], kb_sb[:, 0:512],
                                                  kb_sb[:, 0:512], ALU.mult),
                    ])
                    yield
                    yield
                    t_ss = em("T", [("V", v_kb)], lambda t, Y=Y: t.matmul(
                        Y[0:2, :], h2_b[:], ksq[:, 0:512], start=True, stop=True),
                        inc="T")
                    v_st = em("V", [("T", t_ss)], lambda v, Y=Y: v.tensor_scalar(
                        ss1[:, 0:512], Y[0:2, :],
                        1.0 / HD, EPS_RMS, ALU.mult, ALU.add), inc="V")
                    s_ln = em("S", [("V", v_st)], lambda s: s.activation(
                        ss2[:, 0:512], ss1[:, 0:512], AF.Ln), inc="S")
                    s_ex = em("S", [("S", s_ln)], lambda s: s.activation(
                        rrow[:, 0:512], ss2[:, 0:512], AF.Exp, scale=-0.5),
                        inc="S")
                    yield
                    yield
                    t_rx = em("T", [("S", s_ex)], lambda t, Y=Y, ne=ne: t.matmul(
                        Y, ne[:], rrow[:, 0:512], start=True, stop=True), inc="T")
                    proj_done[seq] = chain("V", [("T", t_rx)], [
                        lambda v, Y=Y: v.tensor_copy(rexp[:, 0:512], Y),
                        lambda v, tc=tc, dst=dst, dof=dof: v.tensor_tensor(
                            dst[:, dof + tc * 512:dof + (tc + 1) * 512],
                            kb_sb[:, 0:512], rexp[:, 0:512], ALU.mult),
                    ])
                    yield
            for tt in range(KT):
                seq = proj_seq[0]; proj_seq[0] += 1
                X = PSL[seq % 2]
                wt = [(wsem, n_w)]
                if seq - 2 in proj_done:
                    wt.append(("V", proj_done[seq - 2]))
                def vmm(t, X=X, woff=woff, tt=tt, c=c):
                    for d in range(DC):
                        t.matmul(X[:, 0:128],
                                 xT[:, d * NKV + tt * 128:d * NKV + (tt + 1) * 128],
                                 Wv_p[:, woff + d * 128:woff + (d + 1) * 128],
                                 start=(d == 0), stop=False)
                    return t.matmul(X[:, 0:128], ones_b[0:1, 0:128],
                                    bv_b[0:1, c * 128:(c + 1) * 128],
                                    start=False, stop=True)
                t_mm = em("T", wt, vmm, inc="T")
                proj_done[seq] = em("V", [("T", t_mm)], lambda v, X=X, tt=tt, voff=voff:
                    v.tensor_copy(
                        Vp[:, voff + tt * 130:voff + tt * 130 + 130].rearrange(
                            "p (two c) -> p two c", two=2)[:, :, 0:64],
                        X[:, 0:128].rearrange("p (two c) -> p two c", two=2)),
                    inc="V")
                yield

        def weight_dmas(c):
            wb = c % 2
            wwait = [("T", proj_T_end[c - 2])] if c - 2 in proj_T_end else []
            gdma(Wq_p[:, wb * DC * 128:(wb + 1) * DC * 128].rearrange(
                "p (d j) -> p d j", d=DC),
                Wq.rearrange("(d p) j -> p d j", p=128)[:, :, c * 128:(c + 1) * 128],
                waits=wwait, s=f"gw{wb}")
            gdma(Wk_p[:, wb * DC * 128:(wb + 1) * DC * 128].rearrange(
                "p (d j) -> p d j", d=DC),
                Wk.rearrange("(d p) j -> p d j", p=128)[:, :, c * 128:(c + 1) * 128],
                s=f"gw{wb}")
            nW[c] = gdma(Wv_p[:, wb * DC * 128:(wb + 1) * DC * 128].rearrange(
                "p (d j) -> p d j", d=DC),
                Wv.rearrange("(d p) j -> p d j", p=128)[:, :, c * 128:(c + 1) * 128],
                s=f"gw{wb}")

        weight_dmas(0)
        weight_dmas(1)
        for _ in proj_units(0):
            pass
        proj_T_end[0] = ct["T"]
        v_proj_done_pair = {0: proj_done[proj_seq[0] - 1]}
        v_og_last = {}

        for c in range(NP):
            kb = (c % 2) * NKV
            qb = (c % 2) * NQ
            vb = (c % 2) * KT * 130
            if c + 1 < NP:
                if c + 1 >= 2:
                    weight_dmas(c + 1)
                pg = proj_units(c + 1)
            else:
                pg = iter(())
            pg_done = False

            v_oev = {}
            pend_pv = []

            def make_pv(gi2, kt2, slot2, qc2, c=c, vb=vb):
                wp = [("S", s_exp_g[gi2])]
                if kt2 == 0 and qc2 - 1 in v_oev:
                    wp.append(("V", v_oev[qc2 - 1]))
                if kt2 == 0 and qc2 == 0 and c - 1 in v_og_last:
                    wp.append(("V", v_og_last[c - 1]))
                def pv(t, kt=kt2, slot=slot2, vb=vb):
                    t.matmul(psO[0:65, 0:512],
                             Vp[:, vb + kt * 130:vb + kt * 130 + 65],
                             Pring[:, slot * 1024:slot * 1024 + 512],
                             start=(kt == 0), stop=(kt == KT - 1))
                    return t.matmul(psO[0:65, 512:1024],
                                    Vp[:, vb + kt * 130 + 65:vb + kt * 130 + 130],
                                    Pring[:, slot * 1024 + 512:(slot + 1) * 1024],
                                    start=(kt == 0), stop=(kt == KT - 1))
                t_pv_g[gi2] = em("T", wp, pv, inc="T")

            for qc in range(NQC):
                for kt in range(KT):
                    gi = gidx[0]; gidx[0] += 1
                    buf = gi % 2
                    slot = gi % PSLOT
                    wt = []
                    if gi - 2 in s_exp_g:
                        wt.append(("S", s_exp_g[gi - 2]))
                    if kt == 0 and qc == 0:
                        wt.append(("V", v_proj_done_pair[c]))
                        if c - 1 in v_og_last:
                            wt.append(("V", v_og_last[c - 1]))
                    def smm(t, buf=buf, kt=kt, qc=qc, kb=kb, qb=qb):
                        t.matmul(psS[:, buf * 1024:buf * 1024 + 512],
                                 K_sb[0:64, kb + kt * 128:kb + (kt + 1) * 128],
                                 Q_sb[0:64, qb + qc * 512:qb + (qc + 1) * 512],
                                 start=True, stop=True, tile_position=(0, 0))
                        return t.matmul(psS[:, buf * 1024 + 512:buf * 1024 + 1024],
                                        K_sb[64:128, kb + kt * 128:kb + (kt + 1) * 128],
                                        Q_sb[64:128, qb + qc * 512:qb + (qc + 1) * 512],
                                        start=True, stop=True, tile_position=(64, 0))
                    t_s = em("T", wt, smm, inc="T")
                    we = [("T", t_s)]
                    s_exp_g[gi] = em("S", we, lambda s, buf=buf, slot=slot: s.activation(
                        Pring[:, slot * 1024:(slot + 1) * 1024],
                        psS[:, buf * 1024:(buf + 1) * 1024],
                        AF.Exp, bias=ecol[:], scale=0.125), inc="S")
                    pend_pv.append((gi, kt, slot, qc))
                    if len(pend_pv) > 2:
                        make_pv(*pend_pv.pop(0))
                    if not pg_done:
                        try:
                            next(pg)
                        except StopIteration:
                            pg_done = True
                            proj_T_end[c + 1] = ct["T"]
                            v_proj_done_pair[c + 1] = proj_done[proj_seq[0] - 1]
                while pend_pv:
                    make_pv(*pend_pv.pop(0))
                def oev(v, qc=qc):
                    v.tensor_copy(Osb[0:64, qc * 512:(qc + 1) * 512], psO[0:64, 0:512])
                    v.tensor_copy(se_fl[0:1, qc * 512:(qc + 1) * 512],
                                  psO[64:65, 0:512])
                    v.tensor_copy(Osb[64:128, qc * 512:(qc + 1) * 512],
                                  psO[0:64, 512:1024])
                    return v.tensor_copy(se_fl[0:1, NQ + qc * 512:NQ + (qc + 1) * 512],
                                         psO[64:65, 512:1024])
                v_oev[qc] = em("V", [("T", t_pv_g[gidx[0] - 1])], oev, inc="V")
            if not pg_done:
                for _ in pg:
                    pass
                proj_T_end[c + 1] = ct["T"]
                v_proj_done_pair[c + 1] = proj_done[proj_seq[0] - 1]

            v_og = {}
            GNUM = [psO[:, 0:512], psS[:, 0:512]]
            GDEN = [psO[:, 512:1024], psS[:, 512:1024]]
            for tc in range(NQC):
                gb = tc % 2
                wt = [("V", v_oev[NQC - 1])]
                if c == 0 and tc == 0:
                    wt.append(("gz", n_gz))
                if tc - 2 in v_og:
                    wt.append(("V", v_og[tc - 2]))
                t_gn = em("T", wt, lambda t, tc=tc, c=c, gb=gb: t.matmul(
                    GNUM[gb], eg_b[:, c * 128:(c + 1) * 128],
                    sgate[0:H, tc * 512:(tc + 1) * 512], start=True, stop=True),
                    inc="T")
                def gden_mm(t, tc=tc, gb=gb):
                    t.matmul(GDEN[gb], e2a_b[:],
                             se_fl[0:1, tc * 512:(tc + 1) * 512],
                             start=True, stop=False)
                    return t.matmul(GDEN[gb], e2b_b[:],
                                    se_fl[0:1, NQ + tc * 512:NQ + (tc + 1) * 512],
                                    start=False, stop=True)
                t_gd = em("T", [("T", t_gn)], gden_mm, inc="T")
                s_gl = em("S", [("T", t_gd)], lambda s, gb=gb: s.activation(
                    Gden[:, gb * 512:(gb + 1) * 512], GDEN[gb], AF.Ln), inc="S")
                s_ge = em("S", [("S", s_gl)], lambda s, gb=gb: s.activation(
                    GdenR[:, gb * 512:(gb + 1) * 512],
                    Gden[:, gb * 512:(gb + 1) * 512], AF.Exp, scale=-1.0), inc="S")
                v_og[tc] = chain("V", [("S", s_ge)], [
                    lambda v, gb=gb: v.tensor_tensor(
                        Gq_sb[:, gb * 512:(gb + 1) * 512], GNUM[gb],
                        GdenR[:, gb * 512:(gb + 1) * 512], ALU.mult),
                    lambda v, tc=tc, c=c, gb=gb: v.tensor_tensor(
                        OgT[:, c * NQ + tc * 512:c * NQ + (tc + 1) * 512],
                        Osb[:, tc * 512:(tc + 1) * 512],
                        Gq_sb[:, gb * 512:(gb + 1) * 512], ALU.mult),
                ])
            v_og_last[c] = max(v_og[NQC - 1], v_og[NQC - 2])
        play()
        es01.close()
        es_r = ExitStack()
        x2T = es_r.enter_context(nc.sbuf_tensor("x2T", [128, DC * NQ], BF16, side="right"))

        es2 = ExitStack()
        Wo_s = es2.enter_context(nc.sbuf_tensor("Wo_s", [128, DC * D], BF16))
        Zt = es2.enter_context(nc.sbuf_tensor("Zt", [128, DC * 512], BF16))
        xres = es2.enter_context(nc.sbuf_tensor("xres", [128, 2 * D], F32))
        pre1 = es2.enter_context(nc.sbuf_tensor("pre1", [128, D], F32))
        x2t = es2.enter_context(nc.sbuf_tensor("x2t", [128, D], F32))
        x2b = es2.enter_context(nc.sbuf_tensor("x2b", [128, D], BF16))
        sqj = es2.enter_context(nc.sbuf_tensor("sqj", [128, D], BF16))
        st_ = es2.enter_context(nc.sbuf_tensor("st_", [128, 16], F32))
        psZ = es2.enter_context(nc.psum_tensor("psZ", [128, 2 * 512], F32))
        psT2 = es2.enter_context(nc.psum_tensor("psT2", [128, 2 * 1024], BF16))
        psU2 = es2.enter_context(nc.psum_tensor("psU2", [128, 2 * 1024], BF16))

        s_zev, v_pre, v_x2e, t_x2t_h, t_tok_last = {}, {}, {}, {}, {}
        s_rs_h, nx2w_h = {}, {}
        nWo = gdma(Wo_s[:].rearrange("p (c j) -> p c j", c=DC),
                   Wo.rearrange("(c p) j -> p c j", p=128))
        v_z0 = em("V", [], lambda v: v.memset(x2t[:], 0.0), inc="V")
        nx2w_h[-1] = ydma(x2_scr[0:1, :], x2t[0:1, :], waits=[("V", v_z0)], s="x2w")
        zseq = [0]
        for tc in range(4):
            for dp in range(DC):
                seq = zseq[0]; zseq[0] += 1
                buf = seq % 2
                wt = [("gsetup", nWo)]
                if seq - 2 in s_zev:
                    wt.append(("S", s_zev[seq - 2]))
                def zmm(t, buf=buf, dp=dp, tc=tc):
                    r = None
                    for d in range(DC):
                        r = t.matmul(psZ[:, buf * 512:(buf + 1) * 512],
                                     Wo_s[:, d * D + dp * 128:d * D + (dp + 1) * 128],
                                     OgT[:, d * NQ + tc * 512:d * NQ + (tc + 1) * 512],
                                     start=(d == 0), stop=(d == DC - 1))
                    return r
                t_z = em("T", wt, zmm, inc="T")
                wz = [("T", t_z)]
                if tc - 1 in t_tok_last:
                    wz.append(("T", t_tok_last[tc - 1]))
                s_zev[seq] = em("S", wz, lambda s, buf=buf, dp=dp: s.activation(
                    Zt[:, dp * 512:(dp + 1) * 512], psZ[:, buf * 512:(buf + 1) * 512],
                    AF.Identity, bias=bo_cs[:, dp:dp + 1]), inc="S")
            s_z_last = s_zev[zseq[0] - 1]
            for q in range(4):
                g = tc * 4 + q
                buf = g % 2
                wx = [("V", v_pre[g - 2])] if g - 2 in v_pre else []
                nxr = ydma(xres[:, buf * D:(buf + 1) * D],
                           xkv[g * 128:(g + 1) * 128, :], waits=wx, s=f"xr{buf}")
                wt = [("S", s_z_last)]
                if g - 2 in v_pre:
                    wt.append(("V", v_pre[g - 2]))
                def ttok(t, buf=buf, q=q):
                    r = None
                    for dp in range(DC):
                        r = t.transpose(
                            psT2[:, buf * 1024 + dp * 128:buf * 1024 + (dp + 1) * 128],
                            Zt[:, dp * 512 + q * 128:dp * 512 + (q + 1) * 128],
                            id_b[:])
                    return r
                t_tok = em("T", wt, ttok, inc="T")
                t_tok_last[tc] = t_tok
                v_pre[g] = em("V", [("T", t_tok), (f"xr{buf}", nxr)],
                              lambda v, buf=buf: v.tensor_tensor(
                                  pre1[:], psT2[:, buf * 1024:(buf + 1) * 1024],
                                  xres[:, buf * D:(buf + 1) * D], ALU.add), inc="V")
                v_st = chain("V", [("V", v_pre[g])], [
                    lambda v: v.tensor_reduce(st_[:, 0:1], pre1[:], AX.X, ALU.add,
                                              negate=True),
                    lambda v: v.scalar_tensor_tensor(sqj[:], pre1[:], 1.0, pre1[:],
                                                     ALU.mult, ALU.mult,
                                                     accum_out=st_[:, 2:3]),
                    lambda v: v.tensor_scalar(st_[:, 1:2], st_[:, 0:1], 1.0 / D,
                                              None, ALU.mult),
                    lambda v: v.tensor_scalar(st_[:, 3:4], st_[:, 2:3], 1.0 / D,
                                              None, ALU.mult),
                    lambda v: v.tensor_tensor(st_[:, 4:5], st_[:, 1:2], st_[:, 1:2],
                                              ALU.mult),
                    lambda v: v.tensor_tensor(st_[:, 5:6], st_[:, 3:4], st_[:, 4:5],
                                              ALU.subtract),
                    lambda v: v.tensor_scalar(st_[:, 6:7], st_[:, 5:6], 1.0, EPS_LN,
                                              ALU.mult, ALU.add),
                    lambda v: v.reciprocal(st_[:, 7:8], st_[:, 6:7]),
                ])
                s_rs = em("S", [("V", v_st)], lambda s: s.activation(
                    st_[:, 8:9], st_[:, 7:8], AF.Sqrt), inc="S")
                s_rs_h[g] = s_rs
                wx2 = [("S", s_rs)]
                if g - 1 in nx2w_h:
                    wx2.append(("x2w", nx2w_h[g - 1]))
                v_x2 = chain("V", wx2, [
                    lambda v: v.tensor_tensor(st_[:, 9:10], st_[:, 1:2], st_[:, 8:9],
                                              ALU.mult),
                    lambda v: v.scalar_tensor_tensor(
                        x2t[:], pre1[:], st_[:, 8:9],
                        st_[:, 9:10].broadcast_to([128, D]), ALU.mult, ALU.add),
                ])
                nx2w_h[g] = ydma(x2_scr[1 + g * 128:1 + (g + 1) * 128, :], x2t[:],
                                 waits=[("V", v_x2)], s="x2w")
                wcb = [("V", v_x2)]
                if g - 1 in t_x2t_h:
                    wcb.append(("T", t_x2t_h[g - 1]))
                v_cb = em("V", wcb, lambda v: v.tensor_copy(x2b[:], x2t[:]), inc="V")
                wt2 = [("V", v_cb)]
                if g - 2 in v_x2e:
                    wt2.append(("V", v_x2e[g - 2]))
                def tx2(t, buf=buf):
                    r = None
                    for d in range(DC):
                        r = t.transpose(
                            psU2[:, buf * 1024 + d * 128:buf * 1024 + (d + 1) * 128],
                            x2b[:, d * 128:(d + 1) * 128], id_b[:])
                    return r
                t_x2t_h[g] = em("T", wt2, tx2, inc="T")
                v_x2e[g] = em("V", [("T", t_x2t_h[g])], lambda v, g=g, buf=buf:
                    v.tensor_copy(
                        x2T[:].rearrange("p (c n) -> p c n", c=DC)[:, :, g * 128:(g + 1) * 128],
                        psU2[:, buf * 1024:(buf + 1) * 1024].rearrange(
                            "p (c n) -> p c n", c=DC)), inc="V")
        em("Y", [("x2w", ct.get("x2w", 0)), ("xr0", ct.get("xr0", 0)),
                 ("xr1", ct.get("xr1", 0))], lambda y: None)
        em("G", [("gsetup", ct.get("gsetup", 0))], lambda g: None)
        play()
        es2.close()
        es_og.close()

        CAP = 512
        CT = CAP // 128
        es3 = ExitStack()
        e16_b = es3.enter_context(nc.sbuf_tensor("e16_b", [E, E * 128], BF16))
        wgu = es3.enter_context(nc.sbuf_tensor("wgu", [128, 2 * 2 * DC * 512], BF16))
        wdn = es3.enter_context(nc.sbuf_tensor("wdn", [128, 2 * 4 * 1024], BF16))
        rlx = es3.enter_context(nc.sbuf_tensor("rlx", [128, 2 * CAP], BF16))
        rowb = es3.enter_context(nc.sbuf_tensor("rowb", [1, 2 * CAP], BF16))
        silu_g = es3.enter_context(nc.sbuf_tensor("silu_g", [128, 2 * 1024], BF16))
        u_p = es3.enter_context(nc.sbuf_tensor("u_p", [128, 2 * 1024], BF16))
        h_p = es3.enter_context(nc.sbuf_tensor("h_p", [128, 2 * 1024], BF16))
        xgb = es3.enter_context(nc.sbuf_tensor("xgb", [128, 4 * 1040], BF16))
        xgT = es3.enter_context(nc.sbuf_tensor("xgT", [128, 2 * DC * CAP], BF16))
        eoF = es3.enter_context(nc.sbuf_tensor("eoF", [128, 2048], BF16))
        eoT = es3.enter_context(nc.sbuf_tensor("eoT", [128, 4 * 1024], F32))

        es3p = ExitStack()
        gwT_s = es3p.enter_context(nc.sbuf_tensor("gwT_s", [128, DC * E], BF16))
        sigS = es3p.enter_context(nc.sbuf_tensor("sigS", [E, NQ], F32))
        R_sb = es3p.enter_context(nc.sbuf_tensor("R_sb", [E, NQ], BF16))
        smsk = es3p.enter_context(nc.sbuf_tensor("smsk", [128, 16], F32))
        rst = es3p.enter_context(nc.sbuf_tensor("rst", [128, 8], F32))
        mskR = es3p.enter_context(nc.sbuf_tensor("mskR", [E, NQ], BF16))
        posF = es3p.enter_context(nc.sbuf_tensor("posF", [E, NQ], F32))
        iot = es3p.enter_context(nc.sbuf_tensor("iot", [128, E], F32))
        msk2 = es3p.enter_context(nc.sbuf_tensor("msk2", [128, 16], F32))
        m1a = es3p.enter_context(nc.sbuf_tensor("m1a", [128, 16 * 16], F32))
        m12a = es3p.enter_context(nc.sbuf_tensor("m12a", [128, 16 * 16], F32))
        rta = es3p.enter_context(nc.sbuf_tensor("rta", [128, 16 * 16], BF16))
        wrk = es3p.enter_context(nc.sbuf_tensor("wrk", [128, 16], F32))
        flt = es3p.enter_context(nc.sbuf_tensor("flt", [128, 32], F32))
        rts = es3p.enter_context(nc.sbuf_tensor("rts", [128, 32], F32))
        xaug = es3p.enter_context(nc.sbuf_tensor("xaug", [128, 4 * 1032], BF16))
        psSc = es3p.enter_context(nc.psum_tensor("psSc", [16, 512], F32))
        psTk = es3p.enter_context(nc.psum_tensor("psTk", [128, 512], F32))
        psTk2 = es3p.enter_context(nc.psum_tensor("psTk2", [128, 512], F32))
        psTkB = es3p.enter_context(nc.psum_tensor("psTkB", [16, 2 * 128], BF16))
        psAh = es3p.enter_context(nc.psum_tensor("psAh", [128, 2048], F32))
        psAh16 = psAh[:].bitcast(BF16)
        fti = es_r.enter_context(nc.sbuf_tensor("fti", [128, 32], mybir.dt.int32,
                                                side="right"))

        ngw = gdma(gwT_s[:].rearrange("p (c j) -> p c j", c=DC),
                   gwT_d.rearrange("(c p) j -> p c j", p=128))
        gdma(e16_b[:], e16_d)
        ne16 = gdma(iot[:], iota_d)

        def preload_w(ex):
            shared = ex == 0
            e = ex - 1
            wb = ex % 2
            srcs = (ws_gate, ws_up, ws_down) if shared else \
                   (w_gate[e], w_up[e], w_down[e])
            gdma(wgu[:, wb * 2 * DC * 512:wb * 2 * DC * 512 + DC * 512].rearrange(
                "p (d j) -> p d j", d=DC),
                srcs[0].rearrange("(d p) j -> p d j", p=128), s=f"ge{wb}")
            gdma(wgu[:, wb * 2 * DC * 512 + DC * 512:(wb + 1) * 2 * DC * 512].rearrange(
                "p (d j) -> p d j", d=DC),
                srcs[1].rearrange("(d p) j -> p d j", p=128), s=f"ge{wb}")
            return gdma(wdn[:, wb * 4 * 1024:(wb + 1) * 4 * 1024].rearrange(
                "p (m j) -> p m j", m=4),
                srcs[2].rearrange("(m p) j -> p m j", p=128), s=f"ge{wb}")
        n_wex_pre = {0: preload_w(0), 1: preload_w(1)}

        cur_pa = [psAh]
        cur_pa16 = [psAh16]
        sh_half = [True]
        t_dn, v_hp_h, s_sl_h, up_h, n_wex = {}, {}, {}, {}, {}
        gu_units = []
        fu = [0]
        pend_dn = []
        v_ra_seq = {}
        gcnt = [0]
        t_xg_h = {}
        dcnt = [0]
        sa_guard = {}
        n_seo = [0]
        v_xg_last = {}
        v_rlx = {}
        n_sxf = [0, 0]

        def unit_guard():
            s = fu[0]; fu[0] += 1
            w = []
            back = 1 if sh_half[0] else 2
            if s - back in v_ra_seq:
                w.append(v_ra_seq[s - back])
            return s, w

        def uslot(s):
            return 0 if sh_half[0] else s % 2

        def emit_down(key):
            ex, u2, slot, s_ = key[:4]
            shared = ex == 0
            wb = ex % 2
            pa, pa16 = key[4], key[5]
            hb = u2 % 2
            whp = [("S", s_sl_h[(ex, u2)]), up_h[(ex, u2)]]
            if len(gu_units) >= 3:
                whp.append(("T", t_dn[gu_units[-3]]))
            v_hp_h[(ex, u2)] = em("V", whp, lambda v, hb=hb: v.tensor_tensor(
                h_p[:, hb * 1024:(hb + 1) * 1024],
                u_p[:, hb * 1024:(hb + 1) * 1024],
                silu_g[:, hb * 1024:(hb + 1) * 1024], ALU.mult), inc="V")
            def dnmm(t, wb=wb, slot=slot, u2=u2, pa=pa):
                r = None
                for hh in range(DC):
                    for mc in range(4):
                        r = t.matmul(
                            pa[:, slot * 2048 + hh * 256:slot * 2048 + (hh + 1) * 256],
                            wdn[:, wb * 4 * 1024 + mc * 1024 + hh * 128:
                                wb * 4 * 1024 + mc * 1024 + (hh + 1) * 128],
                            h_p[:, (u2 % 2) * 1024 + mc * 256:(u2 % 2) * 1024 + (mc + 1) * 256],
                            start=(mc == 0), stop=(mc == 3))
                return r
            t_dn[(ex, u2)] = em("T", [("V", v_hp_h[(ex, u2)])], dnmm, inc="T")
            s_eoF = em("S", [("T", t_dn[(ex, u2)])], lambda s, slot=slot, pa=pa:
                s.activation(eoF[:], pa[:, slot * 2048:(slot + 1) * 2048],
                             AF.Identity), inc="S")
            def eotr(t, slot=slot, pa16=pa16):
                r = None
                for k in range(2):
                    for hh in range(DC):
                        r = t.transpose(
                            pa16[:, slot * 4096 + k * 1024 + hh * 128:
                                 slot * 4096 + k * 1024 + (hh + 1) * 128],
                            eoF[:, hh * 256 + k * 128:hh * 256 + (k + 1) * 128],
                            id_b[:])
                return r
            t_tr = em("T", [("S", s_eoF)], eotr, inc="T")
            dc = dcnt[0]; dcnt[0] += 1
            ep = dc % 2
            weo = [("T", t_tr)]
            if dc - 2 in sa_guard:
                weo.append(sa_guard[dc - 2])
            s_eoT = chain("S", weo, [
                lambda s, slot=slot, ep=ep, pa16=pa16: s.activation(
                    eoT[:, ep * 2048:ep * 2048 + 1024],
                    pa16[:, slot * 4096:slot * 4096 + 1024], AF.Identity),
                lambda s, slot=slot, ep=ep, pa16=pa16: s.activation(
                    eoT[:, ep * 2048 + 1024:ep * 2048 + 2048],
                    pa16[:, slot * 4096 + 1024:slot * 4096 + 2048], AF.Identity),
            ])
            v_ra_seq[s_] = ("S", s_eoT)
            prev_seo = n_seo[0]
            for k in range(2):
                tl = u2 * 2 + k
                if shared:
                    dst_ap = shr_d[1 + tl * 128:1 + (tl + 1) * 128, :]
                else:
                    dst_ap = eoslot[(ex - 1) * CAP + tl * 128:
                                    (ex - 1) * CAP + (tl + 1) * 128, :]
                n_seo[0] = em("Y", [("S", s_eoT), ("seo", prev_seo)],
                    lambda y, ep=ep, k=k, dst_ap=dst_ap: y.dma_start(
                        out=dst_ap,
                        in_=eoT[:, ep * 2048 + k * 1024:ep * 2048 + (k + 1) * 1024]),
                    inc="seo", incby=16)
            sa_guard[dc] = ("seo", n_seo[0])

        def emit_gather(ex, tl):
            e = ex - 1
            wb = ex % 2
            if pend_dn:
                emit_down(pend_dn.pop(0))
            s_, w_ = unit_guard()
            slot = uslot(s_)
            pa, pa16 = cur_pa[0], cur_pa16[0]
            gc = gcnt[0]; gcnt[0] += 1
            gb = gc % 4
            gsem = f"gg{gb}"
            wdma = [("sxp0", n_sxf[0]), ("sxp1", n_sxf[1])]
            if gc - 4 in t_xg_h:
                wdma.append(("T", t_xg_h[gc - 4]))
            ndma = em("G", wdma, lambda g, gb=gb, tl=tl, e=e: g.dma_start(
                out=xgb[:, gb * 1040:gb * 1040 + 1032],
                in_=xslot[e * CAP + tl * 128:e * CAP + (tl + 1) * 128, :]),
                inc=gsem, incby=16)
            def xtr(t, gb=gb, slot=slot, pa16=pa16):
                for d in range(DC):
                    t.transpose(
                        pa16[:, slot * 4096 + d * 128:slot * 4096 + (d + 1) * 128],
                        xgb[:, gb * 1040 + d * 128:gb * 1040 + (d + 1) * 128],
                        id_b[:])
                return t.transpose(
                    pa16[0:1, slot * 4096 + 1024:slot * 4096 + 1152],
                    xgb[:, gb * 1040 + 1024:gb * 1040 + 1025], id_b[:])
            t_xg_h[gc] = em("T", w_ + [(gsem, ndma)], xtr, inc="T")
            v_xg = chain("V", [("T", t_xg_h[gc])], [
                lambda v, wb=wb, tl=tl, slot=slot, pa16=pa16: v.tensor_copy(
                    xgT[:, wb * DC * CAP:(wb + 1) * DC * CAP].rearrange(
                        "p (c n) -> p c n", c=DC)[:, :, tl * 128:(tl + 1) * 128],
                    pa16[:, slot * 4096:slot * 4096 + 1024].rearrange(
                        "p (c n) -> p c n", c=DC)),
                lambda v, wb=wb, tl=tl, slot=slot, pa16=pa16: v.tensor_copy(
                    rowb[0:1, wb * CAP + tl * 128:wb * CAP + (tl + 1) * 128],
                    pa16[0:1, slot * 4096 + 1024:slot * 4096 + 1152]),
            ])
            v_ra_seq[s_] = ("V", v_xg)
            v_xg_last[ex] = v_xg

        def emit_rlx(ex):
            wb = ex % 2
            if pend_dn:
                emit_down(pend_dn.pop(0))
            s_, w_ = unit_guard()
            slot = uslot(s_)
            pa = cur_pa[0]
            w_.append(("V", v_xg_last[ex]))
            t_rl = em("T", w_, lambda t, wb=wb, slot=slot, pa=pa: t.matmul(
                pa[:, slot * 2048:slot * 2048 + CAP],
                ones_b[0:1, 0:128],
                rowb[0:1, wb * CAP:(wb + 1) * CAP], start=True, stop=True), inc="T")
            v_rlx[ex] = em("V", [("T", t_rl)], lambda v, wb=wb, slot=slot, pa=pa:
                v.tensor_copy(rlx[:, wb * CAP:(wb + 1) * CAP],
                              pa[:, slot * 2048:slot * 2048 + CAP]), inc="V")
            v_ra_seq[s_] = ("V", v_rlx[ex])

        def prefetch_for(exn):
            if exn > E:
                return
            for tl in range(CT):
                emit_gather(exn, tl)
            emit_rlx(exn)

        def emit_unit(ex, u2):
            shared = ex == 0
            wb = ex % 2
            if pend_dn:
                emit_down(pend_dn.pop(0))
            s_, w_ = unit_guard()
            slot = uslot(s_)
            pa, pa16 = cur_pa[0], cur_pa16[0]
            hb = u2 % 2
            w_.append((f"ge{wb}", n_wex[ex]))
            if not shared:
                w_.append(("V", v_xg_last[ex]))
                w_.append(("V", v_rlx[ex]))
            def gumm(t, wb=wb, u2=u2, slot=slot, shared=shared, pa=pa):
                r = None
                src_off = (lambda d: d * NQ + u2 * 256) if shared else \
                          (lambda d: wb * DC * CAP + d * CAP + u2 * 256)
                srcT = x2T if shared else xgT
                for mc in range(4):
                    for d in range(DC):
                        t.matmul(
                            pa[:, slot * 2048 + mc * 256:slot * 2048 + (mc + 1) * 256],
                            wgu[:, wb * 2 * DC * 512 + d * 512 + mc * 128:
                                wb * 2 * DC * 512 + d * 512 + (mc + 1) * 128],
                            srcT[:, src_off(d):src_off(d) + 256],
                            start=(d == 0), stop=(d == DC - 1))
                for mc in range(4):
                    for d in range(DC):
                        r = t.matmul(
                            pa[:, slot * 2048 + 1024 + mc * 256:
                               slot * 2048 + 1024 + (mc + 1) * 256],
                            wgu[:, wb * 2 * DC * 512 + DC * 512 + d * 512 + mc * 128:
                                wb * 2 * DC * 512 + DC * 512 + d * 512 + (mc + 1) * 128],
                            srcT[:, src_off(d):src_off(d) + 256],
                            start=(d == 0), stop=(d == DC - 1))
                return r
            t_gu = em("T", w_, gumm, inc="T")
            s_sl_h[(ex, u2)] = em("S", [("T", t_gu)], lambda s, slot=slot, hb=hb, pa=pa:
                s.activation(silu_g[:, hb * 1024:(hb + 1) * 1024],
                             pa[:, slot * 2048:slot * 2048 + 1024], AF.Silu),
                inc="S")
            if shared:
                s_up_ = em("S", [("T", t_gu)], lambda s, slot=slot, hb=hb, pa=pa:
                    s.activation(u_p[:, hb * 1024:(hb + 1) * 1024],
                                 pa[:, slot * 2048 + 1024:slot * 2048 + 2048],
                                 AF.Identity), inc="S")
                up_h[(ex, u2)] = ("S", s_up_)
            else:
                def upb(v, wb=wb, u2=u2, slot=slot, hb=hb, pa=pa):
                    return v.tensor_tensor(
                        u_p[:, hb * 1024:(hb + 1) * 1024].rearrange(
                            "p (m t) -> p m t", m=4),
                        pa[:, slot * 2048 + 1024:slot * 2048 + 2048].rearrange(
                            "p (m t) -> p m t", m=4),
                        rlx[:, wb * CAP + u2 * 256:wb * CAP + (u2 + 1) * 256]
                            .unsqueeze(1).broadcast_to([128, 4, 256]),
                        ALU.mult)
                v_up_ = em("V", [("T", t_gu)], upb, inc="V")
                up_h[(ex, u2)] = ("V", v_up_)
            pend_dn.append((ex, u2, slot, s_, pa, pa16))
            gu_units.append((ex, u2))

        s_scev = {}
        for qc in range(NQC):
            wt = [("gsetup", ne16)]
            if qc - 1 in s_scev:
                wt.append(("S", s_scev[qc - 1]))
            def scmm(t, qc=qc):
                r = None
                for d in range(DC):
                    r = t.matmul(psSc[0:E, :],
                                 gwT_s[:, d * E:(d + 1) * E],
                                 x2T[:, d * NQ + qc * 512:d * NQ + (qc + 1) * 512],
                                 start=(d == 0), stop=(d == DC - 1))
                return r
            t_sc = em("T", wt, scmm, inc="T")
            s_scev[qc] = em("S", [("T", t_sc)], lambda s, qc=qc: s.activation(
                sigS[:, qc * 512:(qc + 1) * 512], psSc[0:E, :],
                AF.Sigmoid), inc="S")
        s_sig_done = s_scev[NQC - 1]

        n_wex[0] = n_wex_pre[0]
        n_wex[1] = n_wex_pre[1]
        shared_q = list(range(8))
        site = [0]

        def pump():
            site[0] += 1
            if shared_q and site[0] % 4 == 0:
                emit_unit(0, shared_q.pop(0))

        tkbank = [psTk, psTk2]

        def tk_ap(g):
            return tkbank[g % 2][:, (g // 2) * 16:(g // 2) * 16 + 16]

        v_r2, v_t2_h, t_tk_h, t_rT_h = {}, {}, {}, {}

        def emit_ttk(g):
            wt = [("S", s_sig_done)]
            if g - 2 in v_t2_h:
                wt.append(("V", v_t2_h[g - 2]))
            t_tk_h[g] = em("T", wt, lambda t, g=g: t.transpose(
                tk_ap(g), sigS[0:E, g * 128:(g + 1) * 128],
                id_f[0:16, 0:16]), inc="T")

        def emit_rt(g):
            wrt = [("V", v_t2_h[g])]
            if g - 1 in v_r2:
                wrt.append(("V", v_r2[g - 1]))
            buf2 = g % 2
            t_rT_h[g] = em("T", wrt, lambda t, buf2=buf2, g=g: t.transpose(
                psTkB[0:16, buf2 * 128:(buf2 + 1) * 128],
                rta[:, g * 16:(g + 1) * 16], id_b[:]), inc="T")
            v_r2[g] = em("V", [("T", t_rT_h[g])], lambda v, g=g, buf2=buf2:
                v.tensor_copy(R_sb[:, g * 128:(g + 1) * 128],
                              psTkB[0:16, buf2 * 128:(buf2 + 1) * 128]), inc="V")

        emit_ttk(0)
        emit_ttk(1)
        for g in range(16):
            pg = tk_ap(g)
            v_t2_h[g] = chain("V", [("T", t_tk_h[g])], [
                lambda v, pg=pg: v.tensor_reduce(rst[:, 0:1], pg, AX.X, ALU.max),
                lambda v, g=g, pg=pg: v.tensor_scalar(m1a[:, g * 16:(g + 1) * 16],
                                          pg, rst[:, 0:1],
                                          None, ALU.is_ge),
                lambda v, g=g, pg=pg: v.scalar_tensor_tensor(smsk[:],
                                                 m1a[:, g * 16:(g + 1) * 16], -1e30,
                                                 pg, ALU.mult, ALU.add),
                lambda v: v.tensor_reduce(rst[:, 1:2], smsk[:], AX.X, ALU.max),
                lambda v, g=g, pg=pg: v.tensor_scalar(m12a[:, g * 16:(g + 1) * 16],
                                          pg, rst[:, 1:2],
                                          None, ALU.is_ge),
                lambda v: v.tensor_tensor(rst[:, 2:3], rst[:, 0:1], rst[:, 1:2],
                                          ALU.add),
                lambda v: v.tensor_scalar(rst[:, 3:4], rst[:, 2:3], 1.0, 1e-20,
                                          ALU.mult, ALU.add),
                lambda v: v.reciprocal(rst[:, 4:5], rst[:, 3:4]),
                lambda v: v.tensor_scalar(rst[:, 5:6], rst[:, 4:5], 2.5, None,
                                          ALU.mult),
                lambda v, g=g, pg=pg: v.scalar_tensor_tensor(rta[:, g * 16:(g + 1) * 16],
                                                 pg, rst[:, 5:6],
                                                 m12a[:, g * 16:(g + 1) * 16],
                                                 ALU.mult, ALU.mult),
            ])
            if g + 2 < 16:
                emit_ttk(g + 2)
            if g - 2 >= 0:
                emit_rt(g - 2)
            pump()
        emit_rt(14)
        emit_rt(15)
        v_route_done = v_r2[15]

        v_msk = em("V", [("V", v_route_done)], lambda v: v.tensor_scalar(
            mskR[:], R_sb[:], 0.0, None, ALU.is_gt), inc="V")
        v_pos = em("V", [("V", v_msk)], lambda v: v.tensor_tensor_scan(
            posF[:], mskR[:], mskR[:], -1.0, ALU.add, ALU.bypass), inc="V")

        v_flt, t_ps_h = {}, {}
        n_ld_h, n_sc_h = {}, {}
        em("V", [], lambda v: v.memset(
            xaug[:].rearrange("p (k c) -> p k c", k=4)[:, :, 1025:1032], 0.0),
            inc="V")

        def emit_sc(g):
            xs = f"sxp{g % 2}"
            for k in range(2):
                sl = (g % 2) * 2 + k
                wrc = [("V", v_flt[g])]
                if g - 2 in n_sc_h:
                    wrc.append((xs, n_sc_h[g - 2]))
                v_rc = em("V", wrc, lambda v, g=g, sl=sl, k=k:
                    v.tensor_copy(xaug[:, sl * 1032 + 1024:sl * 1032 + 1025],
                                  rts[:, 2 * g + k:2 * g + k + 1]), inc="V")
                wsc = [("V", v_rc), (f"xap{g % 2}", n_ld_h[g])]
                if g == 0 and k == 0:
                    wsc.append(("gz", n_gz))
                n_sc_h[g] = em("G", wsc, lambda gg, g=g, k=k, sl=sl:
                    gg.indirect_dma_start(
                        out=xslot,
                        out_offset=IndirectOffsetOnAxis(
                            fti[:, 2 * g + k:2 * g + k + 1], 0),
                        in_=xaug[:, sl * 1032:(sl + 1) * 1032], in_offset=None),
                    inc=xs, incby=16)

        def emit_tps(g):
            wt = [("V", v_pos)]
            if g - 2 in v_flt:
                wt.append(("V", v_flt[g - 2]))
            t_ps_h[g] = em("T", wt, lambda t, g=g: t.transpose(
                tk_ap(g), posF[0:E, g * 128:(g + 1) * 128],
                id_f[0:16, 0:16]), inc="T")

        emit_tps(0)
        emit_tps(1)
        for g in range(16):
            pg = tk_ap(g)
            v_flt[g] = chain("V", [("T", t_ps_h[g]), ("gsetup", ne16)], [
                lambda v, g=g: v.tensor_tensor(msk2[:], m12a[:, g * 16:(g + 1) * 16],
                                               m1a[:, g * 16:(g + 1) * 16],
                                               ALU.subtract),
                lambda v, g=g: v.tensor_tensor(wrk[:], m1a[:, g * 16:(g + 1) * 16],
                                               iot[:], ALU.mult),
                lambda v: v.tensor_reduce(rst[:, 0:1], wrk[:], AX.X, ALU.add),
                lambda v, g=g, pg=pg: v.tensor_tensor(wrk[:], m1a[:, g * 16:(g + 1) * 16],
                                               pg, ALU.mult),
                lambda v: v.tensor_reduce(rst[:, 1:2], wrk[:], AX.X, ALU.add),
                lambda v: v.tensor_scalar(rst[:, 1:2], rst[:, 1:2],
                                          float(CAP - 1), None, ALU.min),
                lambda v, g=g: v.scalar_tensor_tensor(
                    flt[:, 2 * g:2 * g + 1], rst[:, 0:1], float(CAP),
                    rst[:, 1:2], ALU.mult, ALU.add),
                lambda v, g=g: v.tensor_tensor(wrk[:], m1a[:, g * 16:(g + 1) * 16],
                                               rta[:, g * 16:(g + 1) * 16], ALU.mult),
                lambda v, g=g: v.tensor_reduce(rts[:, 2 * g:2 * g + 1], wrk[:],
                                               AX.X, ALU.add),
                lambda v, g=g: v.tensor_tensor(wrk[:], msk2[:], iot[:], ALU.mult),
                lambda v: v.tensor_reduce(rst[:, 2:3], wrk[:], AX.X, ALU.add),
                lambda v, g=g, pg=pg: v.tensor_tensor(wrk[:], msk2[:], pg,
                                               ALU.mult),
                lambda v: v.tensor_reduce(rst[:, 3:4], wrk[:], AX.X, ALU.add),
                lambda v: v.tensor_scalar(rst[:, 3:4], rst[:, 3:4],
                                          float(CAP - 1), None, ALU.min),
                lambda v, g=g: v.scalar_tensor_tensor(
                    flt[:, 2 * g + 1:2 * g + 2], rst[:, 2:3], float(CAP),
                    rst[:, 3:4], ALU.mult, ALU.add),
                lambda v, g=g: v.tensor_tensor(wrk[:], msk2[:],
                                               rta[:, g * 16:(g + 1) * 16], ALU.mult),
                lambda v, g=g: v.tensor_reduce(rts[:, 2 * g + 1:2 * g + 2], wrk[:],
                                               AX.X, ALU.add),
                lambda v, g=g: v.tensor_copy(fti[:, 2 * g:2 * g + 2],
                                             flt[:, 2 * g:2 * g + 2]),
            ])
            if g + 2 < 16:
                emit_tps(g + 2)
            pump()
            for k in range(2):
                sl = (g % 2) * 2 + k
                wl = []
                if k == 0 and g - 2 in n_sc_h:
                    wl.append((f"sxp{g % 2}", n_sc_h[g - 2]))
                n_ld_h[g] = gdma(
                    xaug[:, sl * 1032:sl * 1032 + 1024],
                    x2_scr[1 + g * 128:1 + (g + 1) * 128, :], waits=wl,
                    s=f"xap{g % 2}")
            if g - 1 >= 0:
                emit_sc(g - 1)
        emit_sc(15)
        while shared_q:
            emit_unit(0, shared_q.pop(0))
        while pend_dn:
            emit_down(pend_dn.pop(0))
        n_sxf[0], n_sxf[1] = n_sc_h[14], n_sc_h[15]
        play()
        es3p.close()

        psA = es3.enter_context(nc.psum_tensor("psA", [128, 4096], F32))
        psA16 = psA[:].bitcast(BF16)
        cur_pa[0] = psA
        cur_pa16[0] = psA16
        sh_half[0] = False

        prefetch_for(1)
        for ex in range(1, E + 1):
            e = ex - 1
            wb = ex % 2
            if ex >= 2:
                wwait = [("T", t_dn[(ex - 2, (8 if ex - 2 == 0 else CAP // 256) - 1)])]
                srcs = (w_gate[e], w_up[e], w_down[e])
                gdma(wgu[:, wb * 2 * DC * 512:wb * 2 * DC * 512 + DC * 512].rearrange(
                    "p (d j) -> p d j", d=DC),
                    srcs[0].rearrange("(d p) j -> p d j", p=128), waits=wwait,
                    s=f"ge{wb}")
                gdma(wgu[:, wb * 2 * DC * 512 + DC * 512:(wb + 1) * 2 * DC * 512].rearrange(
                    "p (d j) -> p d j", d=DC),
                    srcs[1].rearrange("(d p) j -> p d j", p=128), s=f"ge{wb}")
                n_wex[ex] = gdma(wdn[:, wb * 4 * 1024:(wb + 1) * 4 * 1024].rearrange(
                    "p (m j) -> p m j", m=4),
                    srcs[2].rearrange("(m p) j -> p m j", p=128), s=f"ge{wb}")
            for u2 in range(CAP // 256):
                emit_unit(ex, u2)
                if u2 == 0 and ex < E:
                    prefetch_for(ex + 1)
        while pend_dn:
            emit_down(pend_dn.pop(0))
        em("G", [("ge0", ct.get("ge0", 0)), ("ge1", ct.get("ge1", 0)),
                 ("gg0", ct.get("gg0", 0)), ("gg1", ct.get("gg1", 0)),
                 ("gg2", ct.get("gg2", 0)), ("gg3", ct.get("gg3", 0)),
                 ("sxp0", ct.get("sxp0", 0)), ("sxp1", ct.get("sxp1", 0))],
           lambda g: None)
        em("Y", [("seo", ct.get("seo", 0))], lambda y: None)
        play()
        es3.close()

        es4 = ExitStack()
        xrs = es4.enter_context(nc.sbuf_tensor("xrs", [128, 8 * D], F32))
        sqj2 = es4.enter_context(nc.sbuf_tensor("sqj2", [128, D], BF16))
        st2 = es4.enter_context(nc.sbuf_tensor("st2", [128, 16], F32))
        ytl = es4.enter_context(nc.sbuf_tensor("ytl", [128, 2 * D], F32))
        x2i_s = es4.enter_context(nc.sbuf_tensor("x2i_s", [128, 16], mybir.dt.int32))
        nx2i = gdma(x2i_s[:], x2i_d)

        v_fin, ny_h = {}, {}
        n_shr_h, n_a1, n_a2, n_a3 = {}, {}, {}, {}
        seo_tot = ct.get("seo", 0)

        def b4_dest(g):
            return xrs[:, (g % 8) * D:(g % 8 + 1) * D]

        def b4_s0(g):
            wx = [("V", v_fin[g - 8])] if g - 8 in v_fin else []
            if g == 0:
                wx.append(("seo", seo_tot))
            n_shr_h[g] = ydma(b4_dest(g), shr_d[1 + g * 128:1 + (g + 1) * 128, :],
                              waits=wx, s=f"yr{g % 8}")

        def b4_s1(g):
            wa1 = [(f"yr{g % 8}", n_shr_h[g]), ("gsetup", nx2i)]
            if g == 0:
                wa1.append(("seo", seo_tot))
            n_a1[g] = em("G", wa1, lambda gg, g=g: gg.indirect_dma_start(
                out=b4_dest(g), out_offset=None, in_=x2_scr,
                in_offset=IndirectOffsetOnAxis(x2i_s[:, g:g + 1], 0),
                compute_op=ALU.add), inc=f"gg{g % 8}", incby=16)

        def b4_s2(g):
            n_a2[g] = em("G", [(f"gg{g % 8}", n_a1[g])], lambda gg, g=g:
                gg.indirect_dma_start(
                    out=b4_dest(g), out_offset=None, in_=eoslot,
                    in_offset=IndirectOffsetOnAxis(fti[:, 2 * g:2 * g + 1], 0),
                    compute_op=ALU.add), inc=f"gg{g % 8}", incby=16)

        def b4_s3(g):
            n_a3[g] = em("G", [(f"gg{g % 8}", n_a2[g])], lambda gg, g=g:
                gg.indirect_dma_start(
                    out=b4_dest(g), out_offset=None, in_=eoslot,
                    in_offset=IndirectOffsetOnAxis(fti[:, 2 * g + 1:2 * g + 2], 0),
                    compute_op=ALU.add), inc=f"gg{g % 8}", incby=16)

        def b4_s4(g):
            dest = b4_dest(g)
            v_s1 = chain("V", [(f"gg{g % 8}", n_a3[g])], [
                lambda v, dest=dest: v.tensor_reduce(
                    st2[:, 0:1], dest, AX.X, ALU.add, negate=True),
                lambda v, dest=dest: v.scalar_tensor_tensor(
                    sqj2[:], dest, 1.0, dest,
                    ALU.mult, ALU.mult, accum_out=st2[:, 2:3]),
                lambda v: v.tensor_scalar(st2[:, 1:2], st2[:, 0:1], 1.0 / D,
                                          None, ALU.mult),
                lambda v: v.tensor_scalar(st2[:, 3:4], st2[:, 2:3], 1.0 / D,
                                          None, ALU.mult),
                lambda v: v.tensor_tensor(st2[:, 4:5], st2[:, 1:2], st2[:, 1:2],
                                          ALU.mult),
                lambda v: v.tensor_tensor(st2[:, 5:6], st2[:, 3:4], st2[:, 4:5],
                                          ALU.subtract),
                lambda v: v.tensor_scalar(st2[:, 6:7], st2[:, 5:6], 1.0, EPS_LN,
                                          ALU.mult, ALU.add),
            ])
            s_rs2 = chain("S", [("V", v_s1)], [
                lambda s: s.activation(st2[:, 7:8], st2[:, 6:7], AF.Ln),
                lambda s: s.activation(st2[:, 8:9], st2[:, 7:8], AF.Exp,
                                       scale=-0.5),
            ])
            wyb = [("S", s_rs2)]
            if g - 2 in ny_h:
                wyb.append((f"yw{g % 2}", ny_h[g - 2]))
            v_fin[g] = chain("V", wyb, [
                lambda v: v.tensor_tensor(st2[:, 9:10], st2[:, 1:2], st2[:, 8:9],
                                          ALU.mult),
                lambda v, g=g, dest=dest: v.scalar_tensor_tensor(
                    ytl[:, (g % 2) * D:(g % 2 + 1) * D], dest,
                    st2[:, 8:9],
                    st2[:, 9:10].broadcast_to([128, D]), ALU.mult, ALU.add),
            ])
            ny_h[g] = ydma(y_out[g * 128:(g + 1) * 128, :],
                           ytl[:, (g % 2) * D:(g % 2 + 1) * D],
                           waits=[("V", v_fin[g])], s=f"yw{g % 2}")

        for t in range(16 + 6):
            if 0 <= t - 6 < 16:
                b4_s4(t - 6)
            if 0 <= t - 5 < 16:
                b4_s3(t - 5)
            if 0 <= t - 3 < 16:
                b4_s2(t - 3)
            if 0 <= t - 1 < 16:
                b4_s1(t - 1)
            if t < 16:
                b4_s0(t)
        em("Y", [("yw0", ct["yw0"]), ("yw1", ct["yw1"])], lambda y: None)
        play()
        es4.close()
        es_r.close()

    nc.finalize()
    return nc


def make_in_maps(inputs):
    f32 = np.float32
    x = np.asarray(inputs["L0_attn_x"], f32)
    Wq = np.asarray(inputs["L0_attn_Wq"], f32)
    bq = np.asarray(inputs["L0_attn_bq"], f32)
    Wk = np.asarray(inputs["L0_attn_Wk"], f32)
    bk = np.asarray(inputs["L0_attn_bk"], f32)
    Wv = np.asarray(inputs["L0_attn_Wv"], f32)
    bv = np.asarray(inputs["L0_attn_bv"], f32)
    Wo = np.asarray(inputs["L0_attn_Wo"], f32)
    bo = np.asarray(inputs["L0_attn_bo"], f32)
    qn = np.asarray(inputs["L0_attn_qn_w"], f32)
    kn = np.asarray(inputs["L0_attn_kn_w"], f32)
    gate_w = np.asarray(inputs["L0_ffn_gate_w"], f32)

    def _ne(w):
        a = np.zeros((2, 128), f32)
        a[0, :64] = w
        a[1, 64:] = w
        return a
    e2a = np.zeros((1, 128), f32); e2a[0, :64] = 1.0
    e2b = np.zeros((1, 128), f32); e2b[0, 64:] = 1.0
    eg = np.zeros((H, NP * 128), f32)
    for c in range(NP):
        eg[2 * c, c * 128:c * 128 + 64] = 1.0
        eg[2 * c + 1, c * 128 + 64:(c + 1) * 128] = 1.0
    h2 = np.zeros((128, 2), f32)
    h2[:64, 0] = 1.0
    h2[64:, 1] = 1.0
    e16 = np.zeros((E, E * 128), f32)
    for e in range(E):
        e16[e, e * 128:(e + 1) * 128] = 1.0

    common = dict(
        Wq=Wq, Wk=Wk, Wv=Wv, Wo=Wo,
        bq_c=np.ascontiguousarray(bq[:D].reshape(DC, 128).T),
        bgate_c=np.ascontiguousarray(bq[D:, None]),
        bk_c=np.ascontiguousarray(bk.reshape(DC, 128).T),
        bv_r=np.ascontiguousarray(bv[None, :]),
        bo_c=np.ascontiguousarray(bo.reshape(DC, 128).T),
        qn_r=np.ascontiguousarray(qn[None, :]),
        kn_r=np.ascontiguousarray(kn[None, :]),
        qne_h=_ne(qn), kne_h=_ne(kn),
        gwT=np.ascontiguousarray(gate_w.T),
        w_gate=np.asarray(inputs["L0_ffn_w_gate"], f32),
        w_up=np.asarray(inputs["L0_ffn_w_up"], f32),
        w_down=np.asarray(inputs["L0_ffn_w_down"], f32),
        ws_gate=np.asarray(inputs["L0_ffn_ws_gate"], f32),
        ws_up=np.asarray(inputs["L0_ffn_ws_up"], f32),
        ws_down=np.asarray(inputs["L0_ffn_ws_down"], f32),
        ident=np.eye(128, dtype=f32),
        ones_r=np.ones((1, 512), f32),
        e2a=e2a, e2b=e2b, eg=eg, e16=e16, h2=h2,
        iota16c=np.tile(np.arange(E, dtype=f32)[None, :], (128, 1)),
        x2i=(1 + np.arange(16, dtype=np.int32)[None, :] * 128
             + np.arange(128, dtype=np.int32)[:, None]),
    )
    in_maps = []
    for c in range(8):
        b, q0 = c // 2, (c % 2) * NQ
        xb = x[b]
        r0 = (q0 + NQ) % NKV
        xkv_c = np.ascontiguousarray(np.concatenate([xb[q0:q0 + NQ], xb[r0:r0 + NQ]]))
        m = dict(common)
        m["xkv"] = xkv_c
        in_maps.append(m)
    return in_maps


def kernel(**inputs):
    from concourse.bass_utils import run_bass_kernel_spmd
    nc = build_nc()
    in_maps = make_in_maps(inputs)
    res = run_bass_kernel_spmd(nc, in_maps, core_ids=list(range(8)))
    y = np.empty((4, NKV, D), np.float32)
    for c in range(8):
        b, q0 = c // 2, (c % 2) * NQ
        y[b, q0:q0 + NQ] = np.asarray(res.results[c]["y"], np.float32)
    return y


